# version 04
# baseline (speedup 1.0000x reference)
import numpy as np
from contextlib import ExitStack

import concourse.bass as bass
import concourse.mybir as mybir
from concourse.bass_utils import run_bass_kernel_spmd

F32 = mybir.dt.float32
BF16 = mybir.dt.bfloat16
I32 = mybir.dt.int32
AF = mybir.ActivationFunctionType
ALU = mybir.AluOpType
AX = mybir.AxisListType

D = 2048
T = 4096
H = 16
HD = 128
DFF = 5504
NDC = D // 128
NFC = DFF // 128
TT = 512
NTT = T // TT
NQG = T // 512
SCALE = 1.0 / np.sqrt(HD)
EPS = 1e-5
NEG = -1.0e30

ENGINES = ("sync", "gpsimd", "scalar", "vector", "tensor")
A1_PARTS = ("q", "k", "v")
A1_SIMPLE_EPI = False
SEM_NAMES = ["LW", "LW1", "LW2", "LX0", "LX1", "LX2", "ST0",
             "PE", "ACT", "DVE", "GP", "FEN"]
SEM_OWNER = {"PE": "tensor", "ACT": "scalar", "DVE": "vector", "GP": "gpsimd"}
STORE_SEMS = ["ST0"]


class Builder:
    def __init__(self, nc):
        self.nc = nc
        self.q = {e: [] for e in ENGINES}
        self.sems = {}
        self.maxwait = {e: {} for e in ENGINES}
        self.stack = ExitStack()
        for s in SEM_NAMES:
            h = self.stack.enter_context(nc.semaphore("s_" + s))
            self.sems[s] = [h, 0]

    def n(self, name):
        return self.sems[name][1]

    def op(self, eng, fn, waits=(), inc=None, fwaits=()):
        ws = []
        for s, v in waits:
            if v <= 0 or SEM_OWNER.get(s) == eng:
                continue
            if self.maxwait[eng].get(s, -1) >= v:
                continue
            self.maxwait[eng][s] = v
            ws.append((self.sems[s][0], v))
        for s, v in fwaits:
            if v <= 0 or self.maxwait[eng].get(s, -1) >= v:
                continue
            self.maxwait[eng][s] = v
            ws.append((self.sems[s][0], v))
        rinc = None
        post = None
        if inc is not None:
            s, d = inc
            self.sems[s][1] += d
            post = self.sems[s][1]
            rinc = (self.sems[s][0], d)
        self.q[eng].append((ws, fn, rinc))
        return post

    def dma(self, out_ap, in_ap, sem, waits=(), eng="sync"):
        def fn(e, o=out_ap, i=in_ap):
            return e.dma_start(out=o, in_=i)
        return self.op(eng, fn, waits=waits, inc=(sem, 16))

    def fence(self):
        for e in ("gpsimd", "scalar", "vector", "tensor"):
            self.op(e, lambda eng: eng.drain(), inc=("FEN", 1))
        tot = self.n("FEN")
        stws = [(s, self.n(s)) for s in STORE_SEMS]
        for e in ENGINES:
            self.op(e, lambda eng: eng.nop(), waits=[("FEN", tot)] + stws)
        return tot

    def flush(self):
        if not any(self.q[e] for e in ENGINES):
            return
        with self.nc.Block() as blk:
            def mk(ename):
                ops = self.q[ename]

                def run(eng):
                    for idx, (ws, fn, inc) in enumerate(ops):
                        for h, v in ws:
                            eng.wait_ge(h, v)
                        try:
                            ins = fn(eng)
                        except Exception:
                            import traceback
                            print(f"FAILING OP: engine={ename} idx={idx}/{len(ops)} "
                                  f"waits={[(h.name, v) for h, v in ws]} inc={inc}")
                            traceback.print_exc()
                            raise
                        if inc is not None:
                            ins.then_inc(inc[0], inc[1])
                return run
            blk.sync(mk("sync"))
            blk.gpsimd(mk("gpsimd"))
            blk.scalar(mk("scalar"))
            blk.vector(mk("vector"))
            blk.tensor(mk("tensor"))
        self.q = {e: [] for e in ENGINES}

    def emit(self):
        self.flush()


def build_program(dbg=(), stages=None):
    nc = bass.Bass("TRN2", target_bir_lowering=False)
    want = (lambda s: True) if stages is None else (lambda s: s in stages)
    b = Builder(nc)
    _uid = [0]

    def sbuf(n_, shape, dt):
        _uid[0] += 1
        return nc.sbuf_tensor(f"{n_}_{_uid[0]}", shape, dt)

    def psum(n_, shape, dt):
        _uid[0] += 1
        return nc.psum_tensor(f"{n_}_{_uid[0]}", shape, dt)

    def dt_in(name, shape, dt=F32):
        return nc.dram_tensor(name, list(shape), dt, kind="ExternalInput")

    XIN = dt_in("XIN", [D, T])
    W0 = {w: dt_in("W0" + w, [D, D]) for w in "QKVO"}
    LUT = dt_in("LUT", [1, 16])
    IDXG = dt_in("IDXG", [DFF, D], I32)
    IDXU = dt_in("IDXU", [DFF, D], I32)
    IDXD = dt_in("IDXD", [D, DFF], I32)
    W1 = {w: dt_in("W1" + w, [D, D]) for w in "QKVO"}
    GQ = dt_in("GQ", [1, HD])
    GK = dt_in("GK", [1, HD])
    WR = dt_in("WR", [D, 8])
    W1G = dt_in("W1G", [D, DFF])
    W1U = dt_in("W1U", [D, DFF])
    W1D = dt_in("W1D", [DFF, D])
    AUGG = dt_in("AUGG", [D, 128])
    AUGU = dt_in("AUGU", [D, 128])
    AUGD = dt_in("AUGD", [DFF, 128])
    BAUGG = dt_in("BAUGG", [128, DFF])
    BAUGU = dt_in("BAUGU", [128, DFF])
    BAUGD = dt_in("BAUGD", [128, D])
    EBIAS = dt_in("EBIAS", [1, 8])
    M2C = dt_in("M2C", [8, 128])
    PERMT = dt_in("PERMT", [128, 128])
    COSQ = dt_in("COSQ", [HD, T])
    SINQ = dt_in("SINQ", [HD, T])
    COSK = dt_in("COSK", [HD, T])
    SINK = dt_in("SINK", [HD, T])

    def scratch(name, shape, dt):
        kind = "ExternalOutput" if name in dbg else "Internal"
        return nc.dram_tensor(name, list(shape), dt, kind=kind)

    OUT = nc.dram_tensor("OUT", [D, T], F32, kind="ExternalOutput")
    XF = scratch("XF", [D, T], F32)
    XB = scratch("XB", [D, T], BF16)
    QS = scratch("QS", [D, T], BF16)
    KS = scratch("KS", [D, T], BF16)
    VS = scratch("VS", [H, 128, T // 128, HD], BF16)
    OS = scratch("OS", [D, T], BF16)
    DH = scratch("DH", [128, NFC, TT], BF16)
    DZ = scratch("DZ", [128, NDC, TT], F32)
    DS = scratch("DS", [128, 1024], F32)
    DP = scratch("DP", [128, 1024], BF16)
    DPN = scratch("DPN", [128, 1024], BF16)
    DPT = scratch("DPT", [128, 8, TT], BF16)
    DNM = scratch("DNM", [2, 128, 1], F32)
    WGT = scratch("WGT", [NFC, 128, NDC + 1, 128], BF16)
    WUT = scratch("WUT", [NFC, 128, NDC + 1, 128], BF16)
    WDT = scratch("WDT", [NDC, 128, NFC + 1, 128], BF16)

    P = b.stack.enter_context

    ident = P(sbuf("ident", [128, 128], BF16))
    ones_col = P(sbuf("ones_col", [128, 1], F32))
    ones_row = P(sbuf("ones_row", [1, 128], F32))
    lut_bc = P(sbuf("lut_bc", [128, 16], F32))
    gq_row = P(sbuf("gq_row", [1, HD], F32))
    gk_row = P(sbuf("gk_row", [1, HD], F32))
    gqs_row = P(sbuf("gqs_row", [1, HD], F32))
    ebias_bc = P(sbuf("ebias_bc", [128, 8], F32))
    m2_sb = P(sbuf("m2_sb", [8, 128], BF16))
    m2_f = P(sbuf("m2_f", [8, 128], F32))
    eps_col = P(sbuf("eps_col", [128, 1], F32))
    perm_sb = P(sbuf("perm_sb", [128, 128], F32))

    def mk_mm(ps, l, r, st, sp):
        return lambda e, ps=ps, l=l, r=r, st=st, sp=sp: e.matmul(
            ps, l, r, start=st, stop=sp)

    def mk_tp(ps, i):
        return lambda e, ps=ps, i=i: e.transpose(ps, i, ident[:])

    def mk_act(o, i, f, bias=0.0, scale=1.0, acc=None):
        return lambda e, o=o, i=i, f=f, bb=bias, ss=scale, aa=acc: e.activation(
            out=o, in_=i, func=f, bias=bb, scale=ss, accum_out=aa)

    def mk_tt(op):
        return lambda o, i0, i1: (
            lambda e, o=o, i0=i0, i1=i1, op=op: e.tensor_tensor(o, i0, i1, op=op))

    def mk_copy(o, i):
        return lambda e, o=o, i=i: e.tensor_copy(o, i)

    def kaccum(psum_ap, pairs, waits):
        n = len(pairs)
        mm = None
        for i, (l, r) in enumerate(pairs):
            mm = b.op("tensor", mk_mm(psum_ap, l, r, i == 0, i == n - 1),
                      waits=waits if i == 0 else (),
                      inc=("PE", 1) if i == n - 1 else None)
        return mm

    if want("PRE"):
        b.op("gpsimd", lambda e: e.memset(ident[:], 0.0), inc=("GP", 1))
        b.op("gpsimd", lambda e: e.affine_select(
            out=ident[:], in_=ident[:], compare_op=ALU.not_equal, fill=1.0,
            base=0, pattern=[[-1, 128]], channel_multiplier=1), inc=("GP", 1))
        b.op("gpsimd", lambda e: e.memset(ones_col[:], 1.0), inc=("GP", 1))
        b.op("gpsimd", lambda e: e.memset(ones_row[:], 1.0), inc=("GP", 1))
        b.op("gpsimd", lambda e: e.memset(eps_col[:], EPS), inc=("GP", 1))
        b.dma(perm_sb[:], PERMT[:], "LW")
        b.dma(lut_bc[:], LUT[0:1, :].partition_broadcast(128), "LW")
        b.dma(gq_row[:], GQ[:], "LW")
        b.dma(gk_row[:], GK[:], "LW")
        b.dma(ebias_bc[:], EBIAS[0:1, :].partition_broadcast(128), "LW")
        b.dma(m2_f[:], M2C[:], "LW")
        b.op("scalar", mk_act(gqs_row[:], gq_row[:], AF.Copy, scale=SCALE),
             waits=[("LW", b.n("LW"))], inc=("ACT", 1))
        b.op("scalar", mk_act(m2_sb[:], m2_f[:], AF.Copy),
             waits=[("LW", b.n("LW"))], inc=("ACT", 1))
        b.fence()
        b.flush()

    if want("INIT"):
        with ExitStack() as st_:
            E = st_.enter_context
            xa = [E(sbuf(f"ini{i}", [128, T], F32)) for i in range(2)]
            xc = [E(sbuf(f"inic{i}", [128, T], BF16)) for i in range(2)]
            cast_done = [0, 0]
            st_done = [0, 0]
            for c in range(NDC):
                s = c % 2
                b.dma(xa[s][:], XIN[128 * c:128 * (c + 1), :], "LX0",
                      waits=[("ACT", cast_done[s])])
                ca = b.op("scalar", mk_act(xc[s][:], xa[s][:], AF.Copy),
                          waits=[("LX0", b.n("LX0")), ("ST0", st_done[s])],
                          inc=("ACT", 1))
                sto = b.dma(XB[128 * c:128 * (c + 1), :], xc[s][:], "ST0",
                            waits=[("ACT", ca)])
                cast_done[s] = ca
                st_done[s] = sto
            b.fence()
            b.flush()

    wtr_state = {"cast": [0, 0], "pe": [0, 0], "ev": [0, 0]}

    def plan_weight_transpose(W, wt_sb, wstg_f, wstg_b, tp_ps, war_pe):
        stx = wtr_state
        for ec in range(NDC):
            s = ec % 2
            b.dma(wstg_f[s][:], W[128 * ec:128 * (ec + 1), :], "LW",
                  waits=[("ACT", stx["cast"][s])])
            ca = b.op("scalar", mk_act(wstg_b[s][:], wstg_f[s][:], AF.Copy),
                      waits=[("LW", b.n("LW")), ("PE", stx["pe"][s])],
                      inc=("ACT", 1))
            stx["cast"][s] = ca
            last_tp = 0
            for dc in range(NDC):
                t_ = dc % 2
                tp = b.op("tensor", mk_tp(tp_ps[t_][:],
                                          wstg_b[s][:, 128 * dc:128 * (dc + 1)]),
                          waits=[("ACT", ca), ("DVE", stx["ev"][t_]),
                                 ("PE", war_pe)],
                          inc=("PE", 1))
                ev = b.op("vector", mk_copy(
                    wt_sb[:, dc, 128 * ec:128 * (ec + 1)], tp_ps[t_][:]),
                    waits=[("PE", tp)], inc=("DVE", 1))
                stx["ev"][t_] = ev
                last_tp = tp
            stx["pe"][s] = last_tp

    def plan_ln_apply(ti, z_sb, s1_ps, s2_ps, bc_ps, rows, a_sb, b_sb,
                      out_f32, out_bf16, xn_sb, xb_sb, state):
        mean_r, ex2_r, var_r, a_r = rows
        c1 = b.op("scalar", mk_act(mean_r[:], s1_ps[:], AF.Copy, scale=1.0 / D),
                  waits=[("PE", state["stats_pe"])], inc=("ACT", 1))
        c2 = b.op("scalar", mk_act(ex2_r[:], s2_ps[:], AF.Copy, scale=1.0 / D),
                  waits=[("PE", state["stats_pe"])], inc=("ACT", 1))
        state["stats_free"] = c2
        d1 = b.op("vector", mk_tt(ALU.mult)(var_r[:], mean_r[:], mean_r[:]),
                  waits=[("ACT", c1)], inc=("DVE", 1))
        d2 = b.op("vector", mk_tt(ALU.subtract)(var_r[:], ex2_r[:], var_r[:]),
                  waits=[("ACT", c2)], fwaits=[("DVE", d1)], inc=("DVE", 1))
        c3 = b.op("scalar", mk_act(a_r[:], var_r[:], AF.Sqrt, bias=eps_col[0:1, :]),
                  waits=[("DVE", d2)], inc=("ACT", 1))
        d3 = b.op("vector", lambda e: e.reciprocal(a_r[:], a_r[:]),
                  waits=[("ACT", c3)], inc=("DVE", 1))
        d4 = b.op("vector", lambda e: e.scalar_tensor_tensor(
            out=var_r[:], in0=mean_r[:], scalar=-1.0, in1=a_r[:],
            op0=ALU.mult, op1=ALU.mult), fwaits=[("DVE", d3)], inc=("DVE", 1))
        p1 = b.op("tensor", mk_mm(bc_ps[0][:], ones_row[:], a_r[:], True, True),
                  waits=[("DVE", d3), ("ACT", state["bc_free"])], inc=("PE", 1))
        p2 = b.op("tensor", mk_mm(bc_ps[1][:], ones_row[:], var_r[:], True, True),
                  waits=[("DVE", d4), ("ACT", state["bc_free"])], inc=("PE", 1))
        e1 = b.op("scalar", mk_act(a_sb[:], bc_ps[0][:], AF.Copy),
                  waits=[("PE", p1)], inc=("ACT", 1))
        e2 = b.op("scalar", mk_act(b_sb[:], bc_ps[1][:], AF.Copy),
                  waits=[("PE", p2)], inc=("ACT", 1))
        state["bc_free"] = e2
        for ec in range(NDC):
            s = ec % 2
            t1 = b.op("vector", mk_tt(ALU.mult)(
                xn_sb[s][:], z_sb[:, ec, :], a_sb[:]),
                waits=[("ACT", e1), ("ST0", state["xn_st"][s])], inc=("DVE", 1))
            t2 = b.op("vector", mk_tt(ALU.add)(xn_sb[s][:], xn_sb[s][:], b_sb[:]),
                      waits=[("ACT", e2)], inc=("DVE", 1))
            sto = b.dma(out_f32[128 * ec:128 * (ec + 1), TT * ti:TT * (ti + 1)],
                        xn_sb[s][:], "ST0", waits=[("DVE", t2)])
            if out_bf16 is not None:
                cb = b.op("scalar", mk_act(xb_sb[s][:], xn_sb[s][:], AF.Copy),
                          waits=[("DVE", t2)], inc=("ACT", 1))
                sto = b.dma(out_bf16[128 * ec:128 * (ec + 1), TT * ti:TT * (ti + 1)],
                            xb_sb[s][:], "ST0", waits=[("ACT", cb)])
            state["xn_st"][s] = sto

    def plan_a1(L):
        Wq, Wk, Wv = ((W0["Q"], W0["K"], W0["V"]) if L == 0
                      else (W1["Q"], W1["K"], W1["V"]))
        with ExitStack() as st_:
            E = st_.enter_context
            wt_sb = E(sbuf("a1_wt", [128, NDC, D], BF16))
            wstg_f = [E(sbuf(f"a1_wf{i}", [128, D], F32)) for i in range(2)]
            wstg_b = [E(sbuf(f"a1_wb{i}", [128, D], BF16)) for i in range(2)]
            xs = [E(sbuf(f"a1_xs{i}", [128, NDC, TT], BF16)) for i in range(2)]
            cos_sb = [E(sbuf(f"a1_cos{i}", [128, TT], F32)) for i in range(2)]
            sin_sb = [E(sbuf(f"a1_sin{i}", [128, TT], F32)) for i in range(2)]
            ep1 = [E(sbuf(f"a1_e1{i}", [128, TT], F32)) for i in range(2)]
            ep2 = [E(sbuf(f"a1_e2{i}", [128, TT], F32)) for i in range(2)]
            qb = [E(sbuf(f"a1_qb{i}", [128, TT], BF16)) for i in range(2)]
            row1 = [E(sbuf(f"a1_r1{i}", [1, TT], F32)) for i in range(2)]
            bcs = [E(sbuf(f"a1_bcs{i}", [128, TT], F32)) for i in range(2)]
            qf = [E(sbuf(f"a1_qf{i}", [128, TT], F32)) for i in range(2)]
            tp_ps = [E(psum(f"a1_tp{i}", [128, 128], BF16)) for i in range(2)]
            mm_ps = [E(psum(f"a1_mm{i}", [128, TT], F32)) for i in range(2)]
            if L == 0:
                rh_ps = [E(psum(f"a1_rh{i}", [128, TT], F32)) for i in range(2)]
                st_ps = bc_ps = None
            else:
                st_ps = [E(psum(f"a1_st{i}", [1, TT], F32)) for i in range(2)]
                bc_ps = [E(psum(f"a1_bc{i}", [128, TT], F32)) for i in range(2)]
                rh_ps = None

            def sweep_qk(W, which):
                plan_weight_transpose(W, wt_sb, wstg_f, wstg_b, tp_ps, b.n("PE"))
                wt_rdy = b.n("DVE")
                OUTQ = QS if which == "q" else KS
                cosT, sinT = (COSQ, SINQ) if which == "q" else (COSK, SINK)
                bank_dve = [0, 0]
                bank_act = [0, 0]
                st_bank_act = [0, 0]
                st_bank_pe = [0, 0]
                rh_free_pe = [0, 0]
                qf_free_dve = [0, 0]
                rhb_free = [0, 0]
                xs_free = [0, 0]
                tab_free = [0, 0]
                qb_st = [0, 0]
                for ti in range(NTT):
                    s = ti % 2
                    b.dma(xs[s][:],
                          XB.rearrange("(c p) t -> p c t", p=128)[:, :, TT * ti:TT * (ti + 1)],
                          "LX0", waits=[("PE", xs_free[s])])
                    ldx = b.n("LX0")
                    if L == 0:
                        b.dma(cos_sb[s][:], cosT[:, TT * ti:TT * (ti + 1)], "LX1",
                              waits=[("DVE", tab_free[s])])
                        b.dma(sin_sb[s][:], sinT[:, TT * ti:TT * (ti + 1)], "LX2",
                              waits=[("DVE", tab_free[s])])
                    ldc, lds = b.n("LX1"), b.n("LX2")
                    for ec in range(NDC):
                        k_ = ec % 2
                        pairs = [(wt_sb[:, kc, 128 * ec:128 * (ec + 1)],
                                  xs[s][:, kc, :]) for kc in range(NDC)]
                        mm = kaccum(mm_ps[k_][:], pairs,
                                    [("DVE", wt_rdy), ("LX0", ldx),
                                     ("DVE", bank_dve[k_]), ("ACT", bank_act[k_])])
                        if A1_SIMPLE_EPI:
                            a3 = b.op("vector", mk_copy(qb[k_][:], mm_ps[k_][:]),
                                      waits=[("PE", mm), ("ST0", qb_st[k_])],
                                      inc=("DVE", 1))
                            bank_dve[k_], bank_act[k_] = a3, 0
                        elif L == 0:
                            ev0 = b.op("scalar", mk_act(qf[k_][:], mm_ps[k_][:],
                                                        AF.Copy),
                                       waits=[("PE", mm),
                                              ("PE", rh_free_pe[k_]),
                                              ("DVE", qf_free_dve[k_])],
                                       inc=("ACT", 1))
                            rh = b.op("tensor", mk_mm(rh_ps[k_][:], perm_sb[:],
                                                      qf[k_][:], True, True),
                                      waits=[("ACT", ev0),
                                             ("DVE", rhb_free[k_]),
                                             ("LW", 16)],
                                      inc=("PE", 1))
                            a1 = b.op("vector", mk_tt(ALU.mult)(
                                ep1[k_][:], qf[k_][:], cos_sb[s][:]),
                                waits=[("ACT", ev0), ("LX1", ldc), ("LX2", lds),
                                       ("ST0", qb_st[k_])], inc=("DVE", 1))
                            a2 = b.op("vector", mk_tt(ALU.mult)(
                                ep2[k_][:], rh_ps[k_][:], sin_sb[s][:]),
                                waits=[("PE", rh)], inc=("DVE", 1))
                            a3 = b.op("vector", mk_tt(ALU.add)(
                                qb[k_][:], ep1[k_][:], ep2[k_][:]),
                                waits=[("DVE", a2)], inc=("DVE", 1))
                            rh_free_pe[k_] = rh
                            qf_free_dve[k_] = a1
                            rhb_free[k_] = a2
                            bank_dve[k_], bank_act[k_] = 0, ev0
                        else:
                            c0 = b.op("scalar", mk_act(ep1[k_][:], mm_ps[k_][:],
                                                       AF.Square),
                                      waits=[("PE", mm), ("ST0", qb_st[k_]),
                                             ("PE", st_bank_pe[k_])],
                                      inc=("ACT", 1))
                            pm = b.op("tensor", mk_mm(st_ps[k_][:], ones_col[:],
                                                      ep1[k_][:], True, True),
                                      waits=[("ACT", c0),
                                             ("ACT", st_bank_act[k_])],
                                      inc=("PE", 1))
                            c1 = b.op("scalar", mk_act(row1[k_][:], st_ps[k_][:],
                                                       AF.Sqrt, scale=1.0 / HD,
                                                       bias=eps_col[0:1, :]),
                                      waits=[("PE", pm)], inc=("ACT", 1))
                            st_bank_act[k_] = c1
                            d1 = b.op("vector",
                                      (lambda k2: lambda e: e.reciprocal(
                                          row1[k2][:], row1[k2][:]))(k_),
                                      waits=[("ACT", c1)], inc=("DVE", 1))
                            grow = gqs_row if which == "q" else gk_row
                            pb2 = b.op("tensor", mk_mm(bc_ps[k_][:], grow[:],
                                                       row1[k_][:], True, True),
                                       waits=[("DVE", d1),
                                              ("DVE", bank_dve[k_])],
                                       inc=("PE", 1))
                            c2 = b.op("scalar", mk_act(bcs[k_][:], bc_ps[k_][:],
                                                       AF.Copy),
                                      waits=[("PE", pb2)], inc=("ACT", 1))
                            a3 = b.op("vector", mk_tt(ALU.mult)(
                                qb[k_][:], mm_ps[k_][:], bcs[k_][:]),
                                waits=[("ACT", c2)], inc=("DVE", 1))
                            st_bank_pe[k_] = pb2
                            bank_dve[k_], bank_act[k_] = a3, c0
                        sto = b.dma(OUTQ[128 * ec:128 * (ec + 1),
                                         TT * ti:TT * (ti + 1)],
                                    qb[k_][:], "ST0", waits=[("DVE", a3)])
                        qb_st[k_] = sto
                    xs_free[s] = b.n("PE")
                    tab_free[s] = b.n("DVE")

            def sweep_v(W):
                plan_weight_transpose(W, wt_sb, wstg_f, wstg_b, tp_ps, b.n("PE"))
                wt_rdy = b.n("DVE")
                bank_act = [0, 0]
                xs_free = [0, 0]
                vb_st = [0, 0]
                for ti in range(NTT):
                    s = ti % 2
                    b.dma(xs[s][:],
                          XB.rearrange("(c p) t -> p c t", p=128)[:, :, TT * ti:TT * (ti + 1)],
                          "LX0", waits=[("PE", xs_free[s])])
                    ldx = b.n("LX0")
                    for tc4 in range(4):
                        tc = 4 * ti + tc4
                        for eg in range(4):
                            k_ = (tc4 * 4 + eg) % 2
                            pairs = [(xs[s][:, kc, 128 * tc4:128 * (tc4 + 1)],
                                      wt_sb[:, kc, 512 * eg:512 * (eg + 1)])
                                     for kc in range(NDC)]
                            mm = kaccum(mm_ps[k_][:], pairs,
                                        [("DVE", wt_rdy), ("LX0", ldx),
                                         ("ACT", bank_act[k_])])
                            ca = b.op("scalar", mk_act(qb[k_][:], mm_ps[k_][:],
                                                       AF.Copy),
                                      waits=[("PE", mm), ("ST0", vb_st[k_])],
                                      inc=("ACT", 1))
                            bank_act[k_] = ca
                            sto = 0
                            for hh in range(4):
                                h = 4 * eg + hh
                                sto = b.dma(VS[h, :, tc, :],
                                            qb[k_][:, 128 * hh:128 * (hh + 1)],
                                            "ST0", waits=[("ACT", ca)])
                            vb_st[k_] = sto
                    xs_free[s] = b.n("PE")

            if "t" in A1_PARTS:
                plan_weight_transpose(Wq, wt_sb, wstg_f, wstg_b, tp_ps, b.n("PE"))
            if "q" in A1_PARTS:
                sweep_qk(Wq, "q")
            if "k" in A1_PARTS:
                sweep_qk(Wk, "k")
            if "v" in A1_PARTS:
                sweep_v(Wv)
            b.fence()
            b.flush()

    def plan_a2(L):
        with ExitStack() as st_:
            E = st_.enter_context
            qh = [E(sbuf(f"a2_q{i}", [128, T], BF16)) for i in range(2)]
            kh = [E(sbuf(f"a2_k{i}", [128, T], BF16)) for i in range(2)]
            vh = [E(sbuf(f"a2_v{i}", [128, T // 128, HD], BF16)) for i in range(2)]
            s4 = E(sbuf("a2_s4", [128, 4, T], F32))
            p_sb = E(sbuf("a2_p", [128, T], BF16))
            pn_sb = [E(sbuf(f"a2_pn{i}", [128, T], BF16)) for i in range(2)]
            pt_sb = E(sbuf("a2_pt", [128, T // 128, TT], BF16))
            ob_sb = [E(sbuf(f"a2_ob{i}", [128, TT], BF16)) for i in range(2)]
            nm = [E(sbuf(f"a2_nm{i}", [128, 1], F32)) for i in range(2)]
            lsum = [E(sbuf(f"a2_l{i}", [128, 1], F32)) for i in range(2)]
            rl = [E(sbuf(f"a2_rl{i}", [128, 1], F32)) for i in range(2)]
            a2m = E(sbuf("a2_m", [128, 4, TT], F32))
            sc_ps = [E(psum(f"a2_sc{i}", [128, TT], F32)) for i in range(2)]
            pt_ps = [E(psum(f"a2_ptp{i}", [128, TT], BF16)) for i in range(2)]
            o_ps = [E(psum(f"a2_o{i}", [128, TT], F32)) for i in range(2)]

            b.op("gpsimd", lambda e: e.memset(a2m[:], 0.0), inc=("GP", 1))
            for d_i in range(4):
                b.op("gpsimd", (lambda di: (lambda e: e.affine_select(
                    out=a2m[:, di, :], in_=a2m[:, di, :], compare_op=ALU.is_ge,
                    fill=NEG, base=128 * di, pattern=[[-1, TT]],
                    channel_multiplier=1)))(d_i), inc=("GP", 1))
            mask_rdy = b.n("GP")
            head_free = [0, 0]
            o_free = [0, 0]
            ob_st = [0, 0]
            sc_free = [0, 0]
            ptb_free = [0, 0]
            tp_done = [0, 0]
            exp_done = [0, 0]
            for h in range(H):
                s = h % 2
                b.dma(qh[s][:], QS[128 * h:128 * (h + 1), :], "LX0",
                      waits=[("PE", head_free[s])])
                ldq = b.n("LX0")
                b.dma(kh[s][:], KS[128 * h:128 * (h + 1), :], "LX1",
                      waits=[("PE", head_free[s])])
                ldk = b.n("LX1")
                b.dma(vh[s][:], VS[h], "LX2", waits=[("PE", head_free[s])])
                ldv = b.n("LX2")
                def emit_scores(g, j):
                    ke = 512 * (g + 1)
                    ev = 0
                    for kt in range(g + 1):
                        k_ = kt % 2
                        mm = b.op("tensor", mk_mm(
                            sc_ps[k_][:],
                            qh[s][:, 512 * g + 128 * j:512 * g + 128 * (j + 1)],
                            kh[s][:, 512 * kt:512 * (kt + 1)], True, True),
                            waits=[("LX0", ldq), ("LX1", ldk),
                                   ("ACT", sc_free[k_])],
                            inc=("PE", 1))
                        ev = b.op("scalar", mk_act(
                            s4[:, j, 512 * kt:512 * (kt + 1)], sc_ps[k_][:],
                            AF.Copy), waits=[("PE", mm)], inc=("ACT", 1))
                        sc_free[k_] = ev
                    return ev

                def emit_softmax(g, j, ev):
                    ke = 512 * (g + 1)
                    jj = j % 2
                    mk_ = b.op("vector", mk_tt(ALU.add)(
                        s4[:, j, 512 * g:512 * (g + 1)],
                        s4[:, j, 512 * g:512 * (g + 1)], a2m[:, j, :]),
                        waits=[("ACT", ev), ("GP", mask_rdy)], inc=("DVE", 1))
                    mx = b.op("vector", (lambda kke, j2, jv: lambda e: e.reduce_max(
                        nm[j2][:], s4[:, jv, 0:kke], axis=AX.X, negate=True))(ke, jj, j),
                        waits=[("ACT", ev), ("ACT", exp_done[jj])],
                        fwaits=[("DVE", mk_)], inc=("DVE", 1))
                    ex = b.op("scalar", (lambda kke, j2, jv: lambda e: e.activation(
                        out=p_sb[:, 0:kke], in_=s4[:, jv, 0:kke], func=AF.Exp,
                        bias=nm[j2][:], accum_out=lsum[j2][:]))(ke, jj, j),
                        waits=[("DVE", mx)], inc=("ACT", 1))
                    exp_done[jj] = ex
                    rc = b.op("vector", (lambda j2: lambda e: e.reciprocal(
                        rl[j2][:], lsum[j2][:]))(jj),
                        waits=[("ACT", ex)], inc=("DVE", 1))
                    pn = b.op("vector", (lambda kke, j2: lambda e: e.tensor_scalar(
                        out=pn_sb[j2][:, 0:kke], in0=p_sb[:, 0:kke],
                        scalar1=rl[j2][:], scalar2=None, op0=ALU.mult))(ke, jj),
                        waits=[("PE", tp_done[jj])], fwaits=[("DVE", rc)],
                        inc=("DVE", 1))
                    for kb in range(g + 1):
                        t_ = kb % 2
                        tpl = None
                        for q4 in range(4):
                            kc = 4 * kb + q4
                            tpl = b.op("tensor", mk_tp(
                                pt_ps[t_][:, 128 * q4:128 * (q4 + 1)],
                                pn_sb[jj][:, 128 * kc:128 * (kc + 1)]),
                                waits=([("DVE", pn), ("ACT", ptb_free[t_])]
                                       if q4 == 0 else ()),
                                inc=("PE", 1) if q4 == 3 else None)
                        pv = b.op("scalar", mk_act(
                            pt_sb[:, 4 * kb:4 * kb + 4, 128 * j:128 * (j + 1)],
                            pt_ps[t_][:].rearrange("p (c x) -> p c x", x=128),
                            AF.Copy), waits=[("PE", tpl)], inc=("ACT", 1))
                        ptb_free[t_] = pv
                    tp_done[jj] = b.n("PE")

                def emit_av(g):
                    ke = 512 * (g + 1)
                    nkc = ke // 128
                    pt_rdy = b.n("ACT")
                    og = g % 2
                    pairs = [(vh[s][:, kc, :], pt_sb[:, kc, :]) for kc in range(nkc)]
                    mm = kaccum(o_ps[og][:], pairs,
                                [("ACT", pt_rdy), ("LX2", ldv), ("ACT", o_free[og])])
                    oc = b.op("scalar", mk_act(ob_sb[og][:], o_ps[og][:], AF.Copy),
                              waits=[("PE", mm), ("ST0", ob_st[og])], inc=("ACT", 1))
                    o_free[og] = oc
                    sto = b.dma(OS[128 * h:128 * (h + 1), 512 * g:512 * (g + 1)],
                                ob_sb[og][:], "ST0", waits=[("ACT", oc)])
                    ob_st[og] = sto

                pend = None
                for g in range(NQG):
                    for j in range(4):
                        ev = emit_scores(g, j)
                        if pend is not None:
                            emit_softmax(*pend)
                            if pend[1] == 3:
                                emit_av(pend[0])
                        pend = (g, j, ev)
                emit_softmax(*pend)
                emit_av(NQG - 1)
                head_free[s] = b.n("PE")
            b.fence()
            b.flush()

    def plan_a3(L):
        Wo = W0["O"] if L == 0 else W1["O"]
        resid = XIN if L == 0 else XF
        with ExitStack() as st_:
            E = st_.enter_context
            wt_sb = E(sbuf("a3_wt", [128, NDC, D], BF16))
            wstg_f = [E(sbuf(f"a3_wf{i}", [128, D], F32)) for i in range(2)]
            wstg_b = [E(sbuf(f"a3_wb{i}", [128, D], BF16)) for i in range(2)]
            os_t = [E(sbuf(f"a3_os{i}", [128, NDC, TT], BF16)) for i in range(2)]
            xr = [E(sbuf(f"a3_xr{i}", [128, TT], F32)) for i in range(2)]
            z_sb = E(sbuf("a3_z", [128, NDC, TT], F32))
            zsq = [E(sbuf(f"a3_zs{i}", [128, TT], F32)) for i in range(2)]
            rows = [E(sbuf(f"a3_row{i}", [1, TT], F32)) for i in range(4)]
            a_sb = E(sbuf("a3_a", [128, TT], F32))
            b_sb = E(sbuf("a3_b", [128, TT], F32))
            xn_sb = [E(sbuf(f"a3_xn{i}", [128, TT], F32)) for i in range(2)]
            xb_sb = [E(sbuf(f"a3_xb{i}", [128, TT], BF16)) for i in range(2)]
            tp_ps = [E(psum(f"a3_tp{i}", [128, 128], BF16)) for i in range(2)]
            y_ps = [E(psum(f"a3_y{i}", [128, TT], F32)) for i in range(2)]
            s1_ps = E(psum("a3_s1", [1, TT], F32))
            s2_ps = E(psum("a3_s2", [1, TT], F32))
            bc_ps = [E(psum(f"a3_bc{i}", [128, TT], F32)) for i in range(2)]

            plan_weight_transpose(Wo, wt_sb, wstg_f, wstg_b, tp_ps, b.n("PE"))
            wt_rdy = b.n("DVE")
            ln_state = {"stats_pe": 0, "stats_free": 0, "bc_free": 0,
                        "xn_st": [0, 0]}
            os_free = [0, 0]
            xr_free = [0, 0]
            bank_dve = [0, 0]
            zsq_free = [0, 0]
            for ti in range(NTT):
                s = ti % 2
                b.dma(os_t[s][:],
                      OS.rearrange("(c p) t -> p c t", p=128)[:, :, TT * ti:TT * (ti + 1)],
                      "LX0", waits=[("PE", os_free[s])])
                ldo = b.n("LX0")
                for ec in range(NDC):
                    k_ = ec % 2
                    b.dma(xr[k_][:],
                          resid[128 * ec:128 * (ec + 1), TT * ti:TT * (ti + 1)],
                          "LX1", waits=[("DVE", xr_free[k_])])
                    ldr = b.n("LX1")
                    pairs = [(wt_sb[:, kc, 128 * ec:128 * (ec + 1)],
                              os_t[s][:, kc, :]) for kc in range(NDC)]
                    mm = kaccum(y_ps[k_][:], pairs,
                                [("DVE", wt_rdy), ("LX0", ldo),
                                 ("DVE", bank_dve[k_])])
                    zz = b.op("vector", mk_tt(ALU.add)(
                        z_sb[:, ec, :], y_ps[k_][:], xr[k_][:]),
                        waits=[("PE", mm), ("LX1", ldr)], inc=("DVE", 1))
                    bank_dve[k_] = zz
                    xr_free[k_] = zz
                    sq = b.op("scalar", mk_act(zsq[k_][:], z_sb[:, ec, :],
                                               AF.Square),
                              waits=[("DVE", zz), ("PE", zsq_free[k_])],
                              inc=("ACT", 1))
                    b.op("tensor", mk_mm(s1_ps[:], ones_col[:], z_sb[:, ec, :],
                                         ec == 0, ec == NDC - 1),
                         waits=[("DVE", zz), ("ACT", ln_state["stats_free"])],
                         inc=("PE", 1))
                    p2 = b.op("tensor", mk_mm(s2_ps[:], ones_col[:], zsq[k_][:],
                                              ec == 0, ec == NDC - 1),
                              waits=[("ACT", sq),
                                     ("ACT", ln_state["stats_free"])],
                              inc=("PE", 1))
                    zsq_free[k_] = p2
                    ln_state["stats_pe"] = p2
                plan_ln_apply(ti, z_sb, s1_ps, s2_ps, bc_ps, rows, a_sb, b_sb,
                              XF, XB, xn_sb, xb_sb, ln_state)
                os_free[s] = b.n("PE")
            b.fence()
            b.flush()

    def plan_f1():
        jobs = [(IDXG, WGT, NFC, D, NDC), (IDXU, WUT, NFC, D, NDC),
                (IDXD, WDT, NDC, DFF, NFC)]
        with ExitStack() as st_:
            E = st_.enter_context
            ix_i = [E(sbuf(f"f1_ix{i}", [128, DFF], I32)) for i in range(2)]
            ix_b = [E(sbuf(f"f1_ib{i}", [128, DFF], BF16)) for i in range(2)]
            acc = E(sbuf("f1_acc", [128, DFF], BF16))
            tmp = E(sbuf("f1_tmp", [128, DFF], BF16))
            acc2 = E(sbuf("f1_acc2", [128, DFF], BF16))
            tmp2 = E(sbuf("f1_tmp2", [128, DFF], BF16))
            stg = [E(sbuf(f"f1_st{i}", [128, NFC, 128], BF16)) for i in range(2)]
            tp_ps = [E(psum(f"f1_tp{i}", [128, TT], BF16)) for i in range(2)]
            ld_free = [0, 0]
            cast_free = [0, 0]
            stg_st = [0, 0]
            ptb_free = [0, 0]
            acc_rd = 0
            acc2_rd = [0]
            for IDX, WT_, nR, Wd, nCC in jobs:
                for rc in range(nR):
                    s = rc % 2
                    b.dma(ix_i[s][:, 0:Wd], IDX[128 * rc:128 * (rc + 1), :], "LW",
                          waits=[("ACT", ld_free[s])])
                    ca = b.op("scalar", mk_act(ix_b[s][:, 0:Wd], ix_i[s][:, 0:Wd],
                                               AF.Copy),
                              waits=[("LW", b.n("LW")), ("DVE", cast_free[s])],
                              inc=("ACT", 1))
                    ld_free[s] = ca
                    g0 = b.op("vector", (lambda ss, ww: lambda e: e.tensor_scalar(
                        out=acc[:, 0:ww], in0=ix_b[ss][:, 0:ww], scalar1=0.0,
                        scalar2=lut_bc[:, 0:1], op0=ALU.is_equal,
                        op1=ALU.mult))(s, Wd),
                        waits=[("ACT", ca), ("PE", acc_rd)], inc=("DVE", 1))
                    gp0 = b.op("gpsimd", (lambda ss, ww: lambda e: e.tensor_scalar(
                        out=acc2[:, 0:ww], in0=ix_b[ss][:, 0:ww], scalar1=10.0,
                        scalar2=lut_bc[:, 10:11], op0=ALU.is_equal,
                        op1=ALU.mult))(s, Wd),
                        waits=[("ACT", ca), ("DVE", acc2_rd[0])], inc=("GP", 1))
                    for jv in range(1, 10):
                        b.op("vector", (lambda ss, jj, ww: lambda e: e.tensor_scalar(
                            out=tmp[:, 0:ww], in0=ix_b[ss][:, 0:ww],
                            scalar1=float(jj), scalar2=lut_bc[:, jj:jj + 1],
                            op0=ALU.is_equal, op1=ALU.mult))(s, jv, Wd),
                            inc=("DVE", 1))
                        g0 = b.op("vector", (lambda ww: lambda e: e.tensor_add(
                            acc[:, 0:ww], acc[:, 0:ww], tmp[:, 0:ww]))(Wd),
                            inc=("DVE", 1))
                    for jv in range(11, 16):
                        b.op("gpsimd", (lambda ss, jj, ww: lambda e: e.tensor_scalar(
                            out=tmp2[:, 0:ww], in0=ix_b[ss][:, 0:ww],
                            scalar1=float(jj), scalar2=lut_bc[:, jj:jj + 1],
                            op0=ALU.is_equal, op1=ALU.mult))(s, jv, Wd),
                            inc=("GP", 1))
                        gp0 = b.op("gpsimd", (lambda ww: lambda e: e.tensor_add(
                            acc2[:, 0:ww], acc2[:, 0:ww], tmp2[:, 0:ww]))(Wd),
                            inc=("GP", 1))
                    g0 = b.op("vector", (lambda ww: lambda e: e.tensor_add(
                        acc[:, 0:ww], acc[:, 0:ww], acc2[:, 0:ww]))(Wd),
                        waits=[("GP", gp0)], inc=("DVE", 1))
                    acc2_rd[0] = g0
                    cast_free[s] = g0
                    pv = 0
                    for kb in range((nCC + 3) // 4):
                        t_ = kb % 2
                        n4 = min(4, nCC - 4 * kb)
                        tpl = None
                        for q4 in range(n4):
                            cc = 4 * kb + q4
                            tpl = b.op("tensor", mk_tp(
                                tp_ps[t_][:, 128 * q4:128 * (q4 + 1)],
                                acc[:, 128 * cc:128 * (cc + 1)]),
                                waits=([("DVE", g0), ("ACT", ptb_free[t_])]
                                       if q4 == 0 else ()),
                                inc=("PE", 1) if q4 == n4 - 1 else None)
                        pv = b.op("scalar", mk_act(
                            stg[s][:, 4 * kb:4 * kb + n4, :],
                            tp_ps[t_][:, 0:128 * n4].rearrange(
                                "p (c x) -> p c x", x=128),
                            AF.Copy),
                            waits=[("PE", tpl), ("ST0", stg_st[s])],
                            inc=("ACT", 1))
                        ptb_free[t_] = pv
                    acc_rd = b.n("PE")
                    sto = b.dma(WT_[rc][:, 0:nCC, :], stg[s][:, 0:nCC, :], "ST0",
                                waits=[("ACT", pv)])
                    stg_st[s] = sto
            b.fence()
            b.flush()

    def plan_f0():
        jobs = [(W1G, WGT, NDC, DFF), (W1U, WUT, NDC, DFF), (W1D, WDT, NFC, D),
                (BAUGG, WGT, None, DFF), (BAUGU, WUT, None, DFF),
                (BAUGD, WDT, None, D)]
        with ExitStack() as st_:
            E = st_.enter_context
            cf = [E(sbuf(f"f0_f{i}", [128, DFF], F32)) for i in range(2)]
            cb = [E(sbuf(f"f0_b{i}", [128, DFF], BF16)) for i in range(2)]
            ld_free = [0, 0]
            cb_st = [0, 0]
            for W_, WT_, nKC, Wd in jobs:
                extra = nKC is None
                kcs = [WT_.shape[2] - 1] if extra else list(range(nKC))
                for kc in kcs:
                    s = kc % 2
                    src_ap = W_[:] if extra else W_[128 * kc:128 * (kc + 1), :]
                    b.dma(cf[s][:, 0:Wd], src_ap, "LW",
                          waits=[("ACT", ld_free[s])])
                    ca = b.op("scalar", mk_act(cb[s][:, 0:Wd], cf[s][:, 0:Wd],
                                               AF.Copy),
                              waits=[("LW", b.n("LW")), ("ST0", cb_st[s])],
                              inc=("ACT", 1))
                    ld_free[s] = ca
                    sto = b.dma(WT_[:, :, kc, :].rearrange("c p x -> p c x"),
                                cb[s][:, 0:Wd].rearrange("p (c x) -> p c x", x=128),
                                "ST0", waits=[("ACT", ca)])
                    cb_st[s] = sto
            b.fence()
            b.flush()

    def plan_f2(L):
        with ExitStack() as st_:
            E = st_.enter_context
            xs = [E(sbuf(f"f2_xs{i}", [128, NDC, TT], BF16)) for i in range(2)]
            wg_t = [E(sbuf(f"f2_wg{i}", [128, NDC + 1, 128], BF16)) for i in range(2)]
            wu_t = [E(sbuf(f"f2_wu{i}", [128, NDC + 1, 128], BF16)) for i in range(2)]
            wd_t = [E(sbuf(f"f2_wd{i}", [128, NFC + 1, 128], BF16)) for i in range(2)]
            h_sb = E(sbuf("f2_h", [128, NFC, TT], BF16))
            sg_sb = [E(sbuf(f"f2_sg{i}", [128, TT], F32)) for i in range(2)]
            xr = [E(sbuf(f"f2_xr{i}", [128, TT], F32)) for i in range(2)]
            z_sb = E(sbuf("f2_z", [128, NDC, TT], F32))
            zsq = sg_sb
            rows = [E(sbuf(f"f2_row{i}", [1, TT], F32)) for i in range(4)]
            a_sb = E(sbuf("f2_a", [128, TT], F32))
            b_sb = E(sbuf("f2_b", [128, TT], F32))
            xn_sb = [E(sbuf(f"f2_xn{i}", [128, TT], F32)) for i in range(2)]
            xb_sb = [E(sbuf(f"f2_xb{i}", [128, TT], BF16)) for i in range(2)]
            gate_ps = [E(psum(f"f2_g{i}", [128, TT], F32)) for i in range(2)]
            up_ps = [E(psum(f"f2_u{i}", [128, TT], F32)) for i in range(2)]
            s1_ps = E(psum("f2_s1", [1, TT], F32))
            s2_ps = E(psum("f2_s2", [1, TT], F32))
            bc_ps = [E(psum(f"f2_bc{i}", [128, TT], F32)) for i in range(2)]

            ln_state = {"stats_pe": 0, "stats_free": 0, "bc_free": 0,
                        "xn_st": [0, 0]}
            if L == 1:
                wr_sb = E(sbuf("f2_wr", [128, NDC, 8], BF16))
                wrf = E(sbuf("f2_wrf", [128, NDC, 8], F32))
                augG = E(sbuf("f2_agG", [128, NDC, 128], BF16))
                augU = E(sbuf("f2_agU", [128, NDC, 128], BF16))
                augD = E(sbuf("f2_agD", [128, NFC, 128], BF16))
                augf = z_sb[:].rearrange("p c t -> p (c t)")[:, 0:NFC * 128]\
                    .rearrange("p (c x) -> p c x", x=128)
                r8 = E(sbuf("f2_r8", [128, 8], F32))
                r8b = E(sbuf("f2_r8b", [128, 8], F32))
                r8c = E(sbuf("f2_r8c", [128, 8], F32))
                m1c = E(sbuf("f2_m1", [128, 1], F32))
                w_bf = E(sbuf("f2_wbf", [128, 8], BF16))
                wt_row = E(sbuf("f2_wtr", [8, TT], BF16))
                w128 = E(sbuf("f2_w128", [128, TT], F32))
                zg_bf = E(sbuf("f2_zg", [128, TT], BF16))
                zu_bf = E(sbuf("f2_zu", [128, TT], BF16))
                zd_bf = E(sbuf("f2_zd", [128, TT], BF16))

                prev = 0
                for dst, stgf, srcap in [
                        (wr_sb[:], wrf[:], WR.rearrange("(c p) x -> p c x", p=128)),
                        (augG[:], augf[:, 0:NDC, :], AUGG.rearrange("(c p) x -> p c x", p=128)),
                        (augU[:], augf[:, 0:NDC, :], AUGU.rearrange("(c p) x -> p c x", p=128)),
                        (augD[:], augf[:], AUGD.rearrange("(c p) x -> p c x", p=128))]:
                    b.dma(stgf, srcap, "LW1", waits=[("ACT", prev)])
                    prev = b.op("scalar", mk_act(dst, stgf, AF.Copy),
                                waits=[("LW1", b.n("LW1"))], inc=("ACT", 1))
                l1_ready = prev
            else:
                l1_ready = 0

            resid = XF
            out_f32 = XF if L == 0 else OUT
            out_bf16 = XB if L == 0 else None
            xs_free = [0, 0]
            wg_free = [0, 0]
            wu_free = [0, 0]
            wdt_free = [0, 0]
            h_free = 0
            bankg_dve = [0, 0]
            bankg_act = [0, 0]
            banku_dve = [0, 0]
            xr_free = [0, 0]
            zsq_free = [0, 0]
            rt_sg_free = 0
            rt_lg_free = 0
            rt_tp_free = 0
            bc0_free_act = 0
            bc1_free_dve = 0
            for ti in range(NTT):
                s = ti % 2
                b.dma(xs[s][:],
                      XB.rearrange("(c p) t -> p c t", p=128)[:, :, TT * ti:TT * (ti + 1)],
                      "LX0", waits=[("PE", xs_free[s])])
                ldx = b.n("LX0")
                if L == 1:
                    wt_ps = bc_ps[1][:].bitcast(BF16)[0:8, 0:128]
                    for c4 in range(4):
                        pairs = [(xs[s][:, kc, 128 * c4:128 * (c4 + 1)],
                                  wr_sb[:, kc, :]) for kc in range(NDC)]
                        mm = kaccum(bc_ps[0][:, 0:8], pairs,
                                    [("LX0", ldx), ("ACT", l1_ready),
                                     ("ACT", rt_lg_free),
                                     ("ACT", ln_state["bc_free"])])
                        sg = b.op("scalar", mk_act(r8[:], bc_ps[0][:, 0:8],
                                                   AF.Sigmoid),
                                  waits=[("PE", mm), ("DVE", rt_sg_free)],
                                  inc=("ACT", 1))
                        rt_lg_free = sg
                        v1 = b.op("vector", mk_tt(ALU.add)(r8b[:], r8[:], ebias_bc[:]),
                                  waits=[("ACT", sg)], inc=("DVE", 1))
                        v2 = b.op("vector", lambda e: e.reduce_max(m1c[:], r8b[:],
                                                                   axis=AX.X),
                                  fwaits=[("DVE", v1)], inc=("DVE", 1))
                        v3 = b.op("vector", lambda e: e.tensor_scalar(
                            out=r8c[:], in0=r8b[:], scalar1=m1c[:], scalar2=None,
                            op0=ALU.is_ge), fwaits=[("DVE", v2)], inc=("DVE", 1))
                        v4 = b.op("vector", lambda e: e.scalar_tensor_tensor(
                            out=r8b[:], in0=r8c[:], scalar=NEG, in1=r8b[:],
                            op0=ALU.mult, op1=ALU.add), fwaits=[("DVE", v3)],
                            inc=("DVE", 1))
                        v5 = b.op("vector", lambda e: e.reduce_max(m1c[:], r8b[:],
                                                                   axis=AX.X),
                                  fwaits=[("DVE", v4)], inc=("DVE", 1))
                        v6 = b.op("vector", lambda e: e.tensor_scalar(
                            out=r8b[:], in0=r8b[:], scalar1=m1c[:], scalar2=None,
                            op0=ALU.is_ge), fwaits=[("DVE", v5)], inc=("DVE", 1))
                        v7 = b.op("vector", mk_tt(ALU.add)(r8c[:], r8c[:], r8b[:]),
                                  fwaits=[("DVE", v6)], inc=("DVE", 1))
                        v8 = b.op("vector", mk_tt(ALU.mult)(r8c[:], r8c[:], r8[:]),
                                  fwaits=[("DVE", v7)], inc=("DVE", 1))
                        rt_sg_free = v8
                        v9 = b.op("vector", lambda e: e.reduce_sum(m1c[:], r8c[:],
                                                                   axis=AX.X),
                                  fwaits=[("DVE", v8)], inc=("DVE", 1))
                        vA = b.op("vector", lambda e: e.tensor_scalar(
                            out=m1c[:], in0=m1c[:], scalar1=1e-8, scalar2=None,
                            op0=ALU.add), fwaits=[("DVE", v9)], inc=("DVE", 1))
                        vB = b.op("vector", lambda e: e.reciprocal(m1c[:], m1c[:]),
                                  fwaits=[("DVE", vA)], inc=("DVE", 1))
                        vC = b.op("vector", lambda e: e.tensor_scalar(
                            out=w_bf[:], in0=r8c[:], scalar1=m1c[:], scalar2=None,
                            op0=ALU.mult), fwaits=[("DVE", vB)], inc=("DVE", 1))
                        tp = b.op("tensor", mk_tp(wt_ps, w_bf[:]),
                                  waits=[("DVE", vC), ("ACT", rt_tp_free),
                                         ("DVE", bc1_free_dve),
                                         ("ACT", ln_state["bc_free"])],
                                  inc=("PE", 1))
                        pv = b.op("scalar", (lambda cc, src: lambda e: e.activation(
                            out=wt_row[:, 128 * cc:128 * (cc + 1)], in_=src,
                            func=AF.Copy))(c4, wt_ps),
                            waits=[("PE", tp)], inc=("ACT", 1))
                        rt_tp_free = pv
                    mmw = b.op("tensor", mk_mm(bc_ps[0][:], m2_sb[:], wt_row[:],
                                               True, True),
                               waits=[("ACT", b.n("ACT"))], inc=("PE", 1))
                    ew = b.op("scalar", mk_act(w128[:], bc_ps[0][:], AF.Copy),
                              waits=[("PE", mmw)], inc=("ACT", 1))
                    bc0_free_act = ew
                    ms = b.op("vector", lambda e: e.memset(w128[0:8, :], 0.125),
                              waits=[("ACT", ew)], inc=("DVE", 1))
                    zc_prev = 0
                    for aug_sb, z_out in ((augG, zg_bf), (augU, zu_bf)):
                        pairs = [(aug_sb[:, kc, :], xs[s][:, kc, :])
                                 for kc in range(NDC)]
                        mm = kaccum(bc_ps[1][:], pairs,
                                    [("ACT", rt_tp_free), ("DVE", bc1_free_dve)])
                        zz = b.op("vector", mk_tt(ALU.mult)(
                            sg_sb[0][:], bc_ps[1][:], w128[:]),
                            waits=[("PE", mm), ("DVE", ms), ("ACT", zc_prev)],
                            inc=("DVE", 1))
                        bc1_free_dve = zz
                        zc_prev = b.op("scalar", mk_act(z_out[:], sg_sb[0][:],
                                                        AF.Copy),
                                       waits=[("DVE", zz)], inc=("ACT", 1))
                    zready = zc_prev
                else:
                    zready = 0
                for dc in range(NFC):
                    k_ = dc % 2
                    b.dma(wg_t[k_][:], WGT[dc], "LW1", waits=[("PE", wg_free[k_])])
                    ldg = b.n("LW1")
                    b.dma(wu_t[k_][:], WUT[dc], "LW2", waits=[("PE", wu_free[k_])])
                    ldu = b.n("LW2")
                    gpairs = [(wg_t[k_][:, kc, :], xs[s][:, kc, :])
                              for kc in range(NDC)]
                    upairs = [(wu_t[k_][:, kc, :], xs[s][:, kc, :])
                              for kc in range(NDC)]
                    if L == 1:
                        gpairs.append((wg_t[k_][:, NDC, :], zg_bf[:]))
                        upairs.append((wu_t[k_][:, NDC, :], zu_bf[:]))
                    mmg = kaccum(gate_ps[k_][:], gpairs,
                                 [("LW1", ldg), ("LX0", ldx),
                                  ("DVE", bankg_dve[k_]), ("ACT", bankg_act[k_]),
                                  ("ACT", zready)])
                    mmu = kaccum(up_ps[k_][:], upairs,
                                 [("LW2", ldu), ("DVE", banku_dve[k_])])
                    sl = b.op("scalar", mk_act(sg_sb[k_][:], gate_ps[k_][:],
                                               AF.Silu),
                              waits=[("PE", mmg)], inc=("ACT", 1))
                    hh = b.op("vector", (lambda dd, k2: lambda e: e.tensor_tensor(
                        out=h_sb[:, dd, :], in0=sg_sb[k2][:], in1=up_ps[k2][:],
                        op=ALU.mult))(dc, k_),
                        waits=[("ACT", sl), ("PE", mmu), ("PE", h_free)],
                        inc=("DVE", 1))
                    wg_free[k_] = mmg
                    wu_free[k_] = mmu
                    bankg_dve[k_] = hh
                    bankg_act[k_] = sl
                    banku_dve[k_] = hh
                h_done = b.n("DVE")
                if "DH" in dbg and ti == 0:
                    b.dma(DH[:], h_sb[:], "ST0", waits=[("DVE", h_done)])
                if L == 1:
                    pairs = [(augD[:, kc, :], h_sb[:, kc, :]) for kc in range(NFC)]
                    mm = kaccum(bc_ps[1][:], pairs,
                                [("DVE", h_done), ("DVE", bc1_free_dve)])
                    zz = b.op("vector", mk_tt(ALU.mult)(
                        sg_sb[0][:], bc_ps[1][:], w128[:]),
                        waits=[("PE", mm), ("DVE", bankg_dve[0])], inc=("DVE", 1))
                    bc1_free_dve = zz
                    zdready = b.op("scalar", mk_act(zd_bf[:], sg_sb[0][:], AF.Copy),
                                   waits=[("DVE", zz)], inc=("ACT", 1))
                else:
                    zdready = 0
                for ec in range(NDC):
                    k_ = ec % 2
                    b.dma(wd_t[k_][:], WDT[ec], "LW1", waits=[("PE", wdt_free[k_])])
                    ldd = b.n("LW1")
                    b.dma(xr[k_][:],
                          resid[128 * ec:128 * (ec + 1), TT * ti:TT * (ti + 1)],
                          "LX1", waits=[("DVE", xr_free[k_])])
                    ldr = b.n("LX1")
                    dpairs = [(wd_t[k_][:, kc, :], h_sb[:, kc, :])
                              for kc in range(NFC)]
                    if L == 1:
                        dpairs.append((wd_t[k_][:, NFC, :], zd_bf[:]))
                    mm = kaccum(gate_ps[k_][:], dpairs,
                                [("LW1", ldd), ("DVE", h_done),
                                 ("DVE", bankg_dve[k_]), ("ACT", bankg_act[k_]),
                                 ("ACT", zdready)])
                    zz = b.op("vector", mk_tt(ALU.add)(
                        z_sb[:, ec, :], gate_ps[k_][:], xr[k_][:]),
                        waits=[("PE", mm), ("LX1", ldr)], inc=("DVE", 1))
                    bankg_dve[k_] = zz
                    bankg_act[k_] = 0
                    xr_free[k_] = zz
                    sq = b.op("scalar", mk_act(zsq[k_][:], z_sb[:, ec, :],
                                               AF.Square),
                              waits=[("DVE", zz), ("PE", zsq_free[k_])],
                              inc=("ACT", 1))
                    b.op("tensor", mk_mm(s1_ps[:], ones_col[:], z_sb[:, ec, :],
                                         ec == 0, ec == NDC - 1),
                         waits=[("DVE", zz), ("ACT", ln_state["stats_free"])],
                         inc=("PE", 1))
                    p2 = b.op("tensor", mk_mm(s2_ps[:], ones_col[:], zsq[k_][:],
                                              ec == 0, ec == NDC - 1),
                              waits=[("ACT", sq),
                                     ("ACT", ln_state["stats_free"])],
                              inc=("PE", 1))
                    zsq_free[k_] = p2
                    ln_state["stats_pe"] = p2
                    wdt_free[k_] = b.n("PE")
                if "DZ" in dbg and ti == 0:
                    b.dma(DZ[:], z_sb[:], "ST0", waits=[("DVE", b.n("DVE"))])
                ln_state["bc_free"] = max(ln_state["bc_free"], bc0_free_act)
                plan_ln_apply(ti, z_sb, s1_ps, s2_ps, bc_ps, rows, a_sb, b_sb,
                              out_f32, out_bf16, xn_sb, xb_sb, ln_state)
                bc1_free_dve = max(bc1_free_dve, 0)
                h_free = b.n("PE")
                xs_free[s] = b.n("PE")
            b.fence()
            b.flush()

    for L in (0, 1):
        if want(f"A1.{L}"):
            plan_a1(L)
        if want(f"A2.{L}"):
            plan_a2(L)
        if want(f"A3.{L}"):
            plan_a3(L)
        if L == 0 and want("F1"):
            plan_f1()
        if L == 1 and want("F0"):
            plan_f0()
        if want(f"F2.{L}"):
            plan_f2(L)

    stws = [(s, b.n(s)) for s in STORE_SEMS]
    b.op("sync", lambda e: e.nop(), waits=stws)
    b.emit()
    b.stack.close()
    return nc


def _host_consts():
    inv = 1.0 / (10000.0 ** (np.arange(0, HD, 2, dtype=np.float64) / HD))
    t = np.arange(T, dtype=np.float64)
    fr = np.outer(t, inv)
    emb = np.concatenate([fr, fr], axis=1)
    cos = np.cos(emb).T.astype(np.float32)
    sin = np.sin(emb).T.astype(np.float32)
    m2 = np.zeros((8, 128), dtype=np.float64)
    for e in range(8):
        m2[e, 64 + 8 * e:64 + 8 * (e + 1)] = 0.125
    permt = np.zeros((128, 128), dtype=np.float32)
    for jj in range(64):
        permt[jj + 64, jj] = -1.0
        permt[jj, jj + 64] = 1.0
    return {
        "PERMT": permt,
        "COSQ": np.ascontiguousarray(cos * np.float32(SCALE)),
        "SINQ": np.ascontiguousarray(sin * np.float32(SCALE)),
        "COSK": np.ascontiguousarray(cos),
        "SINK": np.ascontiguousarray(sin),
        "M2C": m2.astype(np.float32),
    }


def _host_inputs(inputs, core):
    f = np.float32

    def npf(x):
        return np.ascontiguousarray(np.asarray(x, dtype=f))

    def npi(x):
        return np.ascontiguousarray(np.asarray(x, dtype=np.int32))

    zg = np.zeros((D, 56), f)
    zb = np.zeros((56, DFF), f)
    zbd = np.zeros((56, D), f)
    d = {
        "XIN": np.ascontiguousarray(np.asarray(inputs["L0_attn_x"][core], f).T),
        "W0Q": npf(inputs["L0_attn_Wq"]), "W0K": npf(inputs["L0_attn_Wk"]),
        "W0V": npf(inputs["L0_attn_Wv"]), "W0O": npf(inputs["L0_attn_Wo"]),
        "LUT": npf(inputs["L0_ffn_lut"]).reshape(1, 16),
        "IDXG": npi(inputs["L0_ffn_gate_idx"]),
        "IDXU": npi(inputs["L0_ffn_up_idx"]),
        "IDXD": npi(inputs["L0_ffn_down_idx"]),
        "W1Q": npf(inputs["L1_attn_Wq"]), "W1K": npf(inputs["L1_attn_Wk"]),
        "W1V": npf(inputs["L1_attn_Wv"]), "W1O": npf(inputs["L1_attn_Wo"]),
        "GQ": npf(inputs["L1_attn_gq"]).reshape(1, HD),
        "GK": npf(inputs["L1_attn_gk"]).reshape(1, HD),
        "WR": npf(inputs["L1_ffn_W_router"]),
        "W1G": npf(inputs["L1_ffn_W_gate"]), "W1U": npf(inputs["L1_ffn_W_up"]),
        "W1D": npf(inputs["L1_ffn_W_down"]),
        "AUGG": np.ascontiguousarray(np.concatenate(
            [npf(inputs["L1_ffn_sA_gate"]), zg,
             npf(inputs["L1_ffn_A_gate"]).transpose(1, 0, 2).reshape(D, 64)],
            axis=1)),
        "AUGU": np.ascontiguousarray(np.concatenate(
            [npf(inputs["L1_ffn_sA_up"]), zg,
             npf(inputs["L1_ffn_A_up"]).transpose(1, 0, 2).reshape(D, 64)],
            axis=1)),
        "AUGD": np.ascontiguousarray(np.concatenate(
            [npf(inputs["L1_ffn_sA_down"]), np.zeros((DFF, 56), f),
             npf(inputs["L1_ffn_A_down"]).transpose(1, 0, 2).reshape(DFF, 64)],
            axis=1)),
        "BAUGG": np.ascontiguousarray(np.concatenate(
            [npf(inputs["L1_ffn_sB_gate"]), zb,
             npf(inputs["L1_ffn_B_gate"]).reshape(64, DFF)], axis=0)),
        "BAUGU": np.ascontiguousarray(np.concatenate(
            [npf(inputs["L1_ffn_sB_up"]), zb,
             npf(inputs["L1_ffn_B_up"]).reshape(64, DFF)], axis=0)),
        "BAUGD": np.ascontiguousarray(np.concatenate(
            [npf(inputs["L1_ffn_sB_down"]), zbd,
             npf(inputs["L1_ffn_B_down"]).reshape(64, D)], axis=0)),
        "EBIAS": npf(inputs["L1_ffn_expert_bias"]).reshape(1, 8),
    }
    return d


def kernel(**inputs):
    consts = _host_consts()
    nc = build_program()
    in_maps = []
    for c in range(8):
        m = _host_inputs(inputs, c)
        m.update(consts)
        in_maps.append(m)
    res = run_bass_kernel_spmd(nc, in_maps, core_ids=list(range(8)))
    out = np.stack([np.asarray(r["OUT"], dtype=np.float32).T
                    for r in res.results], axis=0)
    return np.ascontiguousarray(out)


# version 05
# speedup vs baseline: 1.6364x; 1.6364x over previous
import numpy as np
from contextlib import ExitStack

import concourse.bass as bass
import concourse.mybir as mybir
from concourse.bass_utils import run_bass_kernel_spmd

F32 = mybir.dt.float32
BF16 = mybir.dt.bfloat16
I32 = mybir.dt.int32
AF = mybir.ActivationFunctionType
ALU = mybir.AluOpType
AX = mybir.AxisListType

D = 2048
T = 4096
H = 16
HD = 128
DFF = 5504
NDC = D // 128
NFC = DFF // 128
TT = 512
NTT = T // TT
NQG = T // 512
SCALE = 1.0 / np.sqrt(HD)
EPS = 1e-5
NEG = -1.0e30

ENGINES = ("sync", "gpsimd", "scalar", "vector", "tensor")
A1_PARTS = ("q", "k", "v")
A1_SIMPLE_EPI = False
SEM_NAMES = ["LW", "LW1", "LW2", "LX0", "LX1", "LX2", "ST0",
             "PE", "ACT", "DVE", "GP", "FEN"]
SEM_OWNER = {"PE": "tensor", "ACT": "scalar", "DVE": "vector", "GP": "gpsimd"}
STORE_SEMS = ["ST0"]


class Builder:
    def __init__(self, nc):
        self.nc = nc
        self.q = {e: [] for e in ENGINES}
        self.sems = {}
        self.maxwait = {e: {} for e in ENGINES}
        self.stack = ExitStack()
        for s in SEM_NAMES:
            h = self.stack.enter_context(nc.semaphore("s_" + s))
            self.sems[s] = [h, 0]

    def n(self, name):
        return self.sems[name][1]

    def op(self, eng, fn, waits=(), inc=None, fwaits=()):
        ws = []
        for s, v in waits:
            if v <= 0 or SEM_OWNER.get(s) == eng:
                continue
            if self.maxwait[eng].get(s, -1) >= v:
                continue
            self.maxwait[eng][s] = v
            ws.append((self.sems[s][0], v))
        for s, v in fwaits:
            if v <= 0 or self.maxwait[eng].get(s, -1) >= v:
                continue
            self.maxwait[eng][s] = v
            ws.append((self.sems[s][0], v))
        rinc = None
        post = None
        if inc is not None:
            s, d = inc
            self.sems[s][1] += d
            post = self.sems[s][1]
            rinc = (self.sems[s][0], d)
        self.q[eng].append((ws, fn, rinc))
        return post

    def dma(self, out_ap, in_ap, sem, waits=(), eng="sync"):
        def fn(e, o=out_ap, i=in_ap):
            return e.dma_start(out=o, in_=i)
        return self.op(eng, fn, waits=waits, inc=(sem, 16))

    def fence(self):
        for e in ("gpsimd", "scalar", "vector", "tensor"):
            self.op(e, lambda eng: eng.drain(), inc=("FEN", 1))
        tot = self.n("FEN")
        stws = [(s, self.n(s)) for s in STORE_SEMS]
        for e in ENGINES:
            self.op(e, lambda eng: eng.nop(), waits=[("FEN", tot)] + stws)
        return tot

    def flush(self):
        if not any(self.q[e] for e in ENGINES):
            return
        with self.nc.Block() as blk:
            def mk(ename):
                ops = self.q[ename]

                def run(eng):
                    for idx, (ws, fn, inc) in enumerate(ops):
                        for h, v in ws:
                            eng.wait_ge(h, v)
                        try:
                            ins = fn(eng)
                        except Exception:
                            import traceback
                            print(f"FAILING OP: engine={ename} idx={idx}/{len(ops)} "
                                  f"waits={[(h.name, v) for h, v in ws]} inc={inc}")
                            traceback.print_exc()
                            raise
                        if inc is not None:
                            ins.then_inc(inc[0], inc[1])
                return run
            blk.sync(mk("sync"))
            blk.gpsimd(mk("gpsimd"))
            blk.scalar(mk("scalar"))
            blk.vector(mk("vector"))
            blk.tensor(mk("tensor"))
        self.q = {e: [] for e in ENGINES}

    def emit(self):
        self.flush()


def build_program(dbg=(), stages=None):
    nc = bass.Bass("TRN2", target_bir_lowering=False)
    want = (lambda s: True) if stages is None else (lambda s: s in stages)
    b = Builder(nc)
    _uid = [0]

    def sbuf(n_, shape, dt):
        _uid[0] += 1
        return nc.sbuf_tensor(f"{n_}_{_uid[0]}", shape, dt)

    def psum(n_, shape, dt):
        _uid[0] += 1
        return nc.psum_tensor(f"{n_}_{_uid[0]}", shape, dt)

    def dt_in(name, shape, dt=F32):
        return nc.dram_tensor(name, list(shape), dt, kind="ExternalInput")

    XIN = dt_in("XIN", [D, T])
    W0 = {w: dt_in("W0" + w, [D, D]) for w in "QKVO"}
    LUT = dt_in("LUT", [1, 16])
    IDXG = dt_in("IDXG", [DFF, D], I32)
    IDXU = dt_in("IDXU", [DFF, D], I32)
    IDXD = dt_in("IDXD", [D, DFF], I32)
    W1 = {w: dt_in("W1" + w, [D, D]) for w in "QKVO"}
    GQ = dt_in("GQ", [1, HD])
    GK = dt_in("GK", [1, HD])
    WR = dt_in("WR", [D, 8])
    W1G = dt_in("W1G", [D, DFF])
    W1U = dt_in("W1U", [D, DFF])
    W1D = dt_in("W1D", [DFF, D])
    AUGG = dt_in("AUGG", [D, 128])
    AUGU = dt_in("AUGU", [D, 128])
    AUGD = dt_in("AUGD", [DFF, 128])
    BAUGG = dt_in("BAUGG", [128, DFF])
    BAUGU = dt_in("BAUGU", [128, DFF])
    BAUGD = dt_in("BAUGD", [128, D])
    EBIAS = dt_in("EBIAS", [1, 8])
    M2C = dt_in("M2C", [8, 128])
    PERMT = dt_in("PERMT", [128, 128])
    COSQ = dt_in("COSQ", [HD, T])
    SINQ = dt_in("SINQ", [HD, T])
    COSK = dt_in("COSK", [HD, T])
    SINK = dt_in("SINK", [HD, T])

    def scratch(name, shape, dt):
        kind = "ExternalOutput" if name in dbg else "Internal"
        return nc.dram_tensor(name, list(shape), dt, kind=kind)

    OUT = nc.dram_tensor("OUT", [D, T], F32, kind="ExternalOutput")
    XF = scratch("XF", [D, T], F32)
    XB = scratch("XB", [D, T], BF16)
    QS = scratch("QS", [D, T], BF16)
    KS = scratch("KS", [D, T], BF16)
    VS = scratch("VS", [H, 128, T // 128, HD], BF16)
    OS = scratch("OS", [D, T], BF16)
    DH = scratch("DH", [128, NFC, TT], BF16)
    DZ = scratch("DZ", [128, NDC, TT], F32)
    DS = scratch("DS", [128, 1024], F32)
    DP = scratch("DP", [128, 1024], BF16)
    DPN = scratch("DPN", [128, 1024], BF16)
    DPT = scratch("DPT", [128, 8, TT], BF16)
    DNM = scratch("DNM", [2, 128, 1], F32)
    WGT = scratch("WGT", [NFC, 128, NDC + 1, 128], BF16)
    WUT = scratch("WUT", [NFC, 128, NDC + 1, 128], BF16)
    WDT = scratch("WDT", [NDC, 128, NFC + 1, 128], BF16)

    P = b.stack.enter_context

    ident = P(sbuf("ident", [128, 128], BF16))
    ones_col = P(sbuf("ones_col", [128, 1], F32))
    ones_row = P(sbuf("ones_row", [1, 128], F32))
    lut_bc = P(sbuf("lut_bc", [128, 16], F32))
    gq_row = P(sbuf("gq_row", [1, HD], F32))
    gk_row = P(sbuf("gk_row", [1, HD], F32))
    gqs_row = P(sbuf("gqs_row", [1, HD], F32))
    ebias_bc = P(sbuf("ebias_bc", [128, 8], F32))
    m2_sb = P(sbuf("m2_sb", [8, 128], BF16))
    m2_f = P(sbuf("m2_f", [8, 128], F32))
    eps_col = P(sbuf("eps_col", [128, 1], F32))
    perm_sb = P(sbuf("perm_sb", [128, 128], F32))

    def mk_mm(ps, l, r, st, sp):
        return lambda e, ps=ps, l=l, r=r, st=st, sp=sp: e.matmul(
            ps, l, r, start=st, stop=sp)

    def mk_tp(ps, i):
        return lambda e, ps=ps, i=i: e.transpose(ps, i, ident[:])

    def mk_act(o, i, f, bias=0.0, scale=1.0, acc=None):
        return lambda e, o=o, i=i, f=f, bb=bias, ss=scale, aa=acc: e.activation(
            out=o, in_=i, func=f, bias=bb, scale=ss, accum_out=aa)

    def mk_tt(op):
        return lambda o, i0, i1: (
            lambda e, o=o, i0=i0, i1=i1, op=op: e.tensor_tensor(o, i0, i1, op=op))

    def mk_copy(o, i):
        return lambda e, o=o, i=i: e.tensor_copy(o, i)

    def kaccum(psum_ap, pairs, waits):
        n = len(pairs)
        mm = None
        for i, (l, r) in enumerate(pairs):
            mm = b.op("tensor", mk_mm(psum_ap, l, r, i == 0, i == n - 1),
                      waits=waits if i == 0 else (),
                      inc=("PE", 1) if i == n - 1 else None)
        return mm

    if want("PRE"):
        b.op("gpsimd", lambda e: e.memset(ident[:], 0.0), inc=("GP", 1))
        b.op("gpsimd", lambda e: e.affine_select(
            out=ident[:], in_=ident[:], compare_op=ALU.not_equal, fill=1.0,
            base=0, pattern=[[-1, 128]], channel_multiplier=1), inc=("GP", 1))
        b.op("gpsimd", lambda e: e.memset(ones_col[:], 1.0), inc=("GP", 1))
        b.op("gpsimd", lambda e: e.memset(ones_row[:], 1.0), inc=("GP", 1))
        b.op("gpsimd", lambda e: e.memset(eps_col[:], EPS), inc=("GP", 1))
        b.dma(perm_sb[:], PERMT[:], "LW")
        b.dma(lut_bc[:], LUT[0:1, :].partition_broadcast(128), "LW")
        b.dma(gq_row[:], GQ[:], "LW")
        b.dma(gk_row[:], GK[:], "LW")
        b.dma(ebias_bc[:], EBIAS[0:1, :].partition_broadcast(128), "LW")
        b.dma(m2_f[:], M2C[:], "LW")
        b.op("scalar", mk_act(gqs_row[:], gq_row[:], AF.Copy, scale=SCALE),
             waits=[("LW", b.n("LW"))], inc=("ACT", 1))
        b.op("scalar", mk_act(m2_sb[:], m2_f[:], AF.Copy),
             waits=[("LW", b.n("LW"))], inc=("ACT", 1))
        b.fence()
        b.flush()

    if want("INIT"):
        with ExitStack() as st_:
            E = st_.enter_context
            xa = [E(sbuf(f"ini{i}", [128, T], F32)) for i in range(2)]
            xc = [E(sbuf(f"inic{i}", [128, T], BF16)) for i in range(2)]
            cast_done = [0, 0]
            st_done = [0, 0]
            for c in range(NDC):
                s = c % 2
                b.dma(xa[s][:], XIN[128 * c:128 * (c + 1), :], "LX0",
                      waits=[("ACT", cast_done[s])])
                ca = b.op("scalar", mk_act(xc[s][:], xa[s][:], AF.Copy),
                          waits=[("LX0", b.n("LX0")), ("ST0", st_done[s])],
                          inc=("ACT", 1))
                sto = b.dma(XB[128 * c:128 * (c + 1), :], xc[s][:], "ST0",
                            waits=[("ACT", ca)])
                cast_done[s] = ca
                st_done[s] = sto
            b.fence()
            b.flush()

    wtr_state = {"cast": [0, 0], "pe": [0, 0], "ev": [0, 0]}

    def plan_weight_transpose(W, wt_sb, wstg_f, wstg_b, tp_ps, war_pe):
        stx = wtr_state
        for ec in range(NDC):
            s = ec % 2
            b.dma(wstg_f[s][:], W[128 * ec:128 * (ec + 1), :], "LW",
                  waits=[("ACT", stx["cast"][s])])
            ca = b.op("scalar", mk_act(wstg_b[s][:], wstg_f[s][:], AF.Copy),
                      waits=[("LW", b.n("LW")), ("PE", stx["pe"][s])],
                      inc=("ACT", 1))
            stx["cast"][s] = ca
            last_tp = 0
            for dc in range(NDC):
                t_ = dc % 2
                tp = b.op("tensor", mk_tp(tp_ps[t_][:],
                                          wstg_b[s][:, 128 * dc:128 * (dc + 1)]),
                          waits=[("ACT", ca), ("DVE", stx["ev"][t_]),
                                 ("PE", war_pe)],
                          inc=("PE", 1))
                ev = b.op("vector", mk_copy(
                    wt_sb[:, dc, 128 * ec:128 * (ec + 1)], tp_ps[t_][:]),
                    waits=[("PE", tp)], inc=("DVE", 1))
                stx["ev"][t_] = ev
                last_tp = tp
            stx["pe"][s] = last_tp

    def plan_ln_apply(ti, z_sb, s1_ps, s2_ps, bc_ps, rows, a_sb, b_sb,
                      out_f32, out_bf16, xn_sb, xb_sb, state):
        mean_r, ex2_r, var_r, a_r = rows
        c1 = b.op("scalar", mk_act(mean_r[:], s1_ps[:], AF.Copy, scale=1.0 / D),
                  waits=[("PE", state["stats_pe"])], inc=("ACT", 1))
        c2 = b.op("scalar", mk_act(ex2_r[:], s2_ps[:], AF.Copy, scale=1.0 / D),
                  waits=[("PE", state["stats_pe"])], inc=("ACT", 1))
        state["stats_free"] = c2
        d1 = b.op("vector", mk_tt(ALU.mult)(var_r[:], mean_r[:], mean_r[:]),
                  waits=[("ACT", c1)], inc=("DVE", 1))
        d2 = b.op("vector", mk_tt(ALU.subtract)(var_r[:], ex2_r[:], var_r[:]),
                  waits=[("ACT", c2)], fwaits=[("DVE", d1)], inc=("DVE", 1))
        c3 = b.op("scalar", mk_act(a_r[:], var_r[:], AF.Sqrt, bias=eps_col[0:1, :]),
                  waits=[("DVE", d2)], inc=("ACT", 1))
        d3 = b.op("vector", lambda e: e.reciprocal(a_r[:], a_r[:]),
                  waits=[("ACT", c3)], inc=("DVE", 1))
        d4 = b.op("vector", lambda e: e.scalar_tensor_tensor(
            out=var_r[:], in0=mean_r[:], scalar=-1.0, in1=a_r[:],
            op0=ALU.mult, op1=ALU.mult), fwaits=[("DVE", d3)], inc=("DVE", 1))
        p1 = b.op("tensor", mk_mm(bc_ps[0][:], ones_row[:], a_r[:], True, True),
                  waits=[("DVE", d3), ("ACT", state["bc_free"])], inc=("PE", 1))
        p2 = b.op("tensor", mk_mm(bc_ps[1][:], ones_row[:], var_r[:], True, True),
                  waits=[("DVE", d4), ("ACT", state["bc_free"])], inc=("PE", 1))
        e1 = b.op("scalar", mk_act(a_sb[:], bc_ps[0][:], AF.Copy),
                  waits=[("PE", p1)], inc=("ACT", 1))
        e2 = b.op("scalar", mk_act(b_sb[:], bc_ps[1][:], AF.Copy),
                  waits=[("PE", p2)], inc=("ACT", 1))
        state["bc_free"] = e2
        for ec in range(NDC):
            s = ec % 2
            t1 = b.op("vector", mk_tt(ALU.mult)(
                xn_sb[s][:], z_sb[:, ec, :], a_sb[:]),
                waits=[("ACT", e1), ("ST0", state["xn_st"][s])], inc=("DVE", 1))
            t2 = b.op("vector", mk_tt(ALU.add)(xn_sb[s][:], xn_sb[s][:], b_sb[:]),
                      waits=[("ACT", e2)], inc=("DVE", 1))
            sto = b.dma(out_f32[128 * ec:128 * (ec + 1), TT * ti:TT * (ti + 1)],
                        xn_sb[s][:], "ST0", waits=[("DVE", t2)])
            if out_bf16 is not None:
                cb = b.op("scalar", mk_act(xb_sb[s][:], xn_sb[s][:], AF.Copy),
                          waits=[("DVE", t2)], inc=("ACT", 1))
                sto = b.dma(out_bf16[128 * ec:128 * (ec + 1), TT * ti:TT * (ti + 1)],
                            xb_sb[s][:], "ST0", waits=[("ACT", cb)])
            state["xn_st"][s] = sto

    def plan_a1(L):
        Wq, Wk, Wv = ((W0["Q"], W0["K"], W0["V"]) if L == 0
                      else (W1["Q"], W1["K"], W1["V"]))
        with ExitStack() as st_:
            E = st_.enter_context
            wt_sb = E(sbuf("a1_wt", [128, NDC, D], BF16))
            wstg_f = [E(sbuf(f"a1_wf{i}", [128, D], F32)) for i in range(2)]
            wstg_b = [E(sbuf(f"a1_wb{i}", [128, D], BF16)) for i in range(2)]
            xs = [E(sbuf(f"a1_xs{i}", [128, NDC, TT], BF16)) for i in range(2)]
            cos_sb = [E(sbuf(f"a1_cos{i}", [128, TT], F32)) for i in range(2)]
            sin_sb = [E(sbuf(f"a1_sin{i}", [128, TT], F32)) for i in range(2)]
            ep1 = [E(sbuf(f"a1_e1{i}", [128, TT], F32)) for i in range(2)]
            ep2 = [E(sbuf(f"a1_e2{i}", [128, TT], F32)) for i in range(2)]
            qb = [E(sbuf(f"a1_qb{i}", [128, TT], BF16)) for i in range(2)]
            row1 = [E(sbuf(f"a1_r1{i}", [1, TT], F32)) for i in range(2)]
            bcs = [E(sbuf(f"a1_bcs{i}", [128, TT], F32)) for i in range(2)]
            qf = [E(sbuf(f"a1_qf{i}", [128, TT], F32)) for i in range(2)]
            tp_ps = [E(psum(f"a1_tp{i}", [128, 128], BF16)) for i in range(2)]
            mm_ps = [E(psum(f"a1_mm{i}", [128, TT], F32)) for i in range(2)]
            if L == 0:
                rh_ps = [E(psum(f"a1_rh{i}", [128, TT], F32)) for i in range(2)]
                st_ps = bc_ps = None
            else:
                st_ps = [E(psum(f"a1_st{i}", [1, TT], F32)) for i in range(2)]
                bc_ps = [E(psum(f"a1_bc{i}", [128, TT], F32)) for i in range(2)]
                rh_ps = None

            def sweep_qk(W, which):
                plan_weight_transpose(W, wt_sb, wstg_f, wstg_b, tp_ps, b.n("PE"))
                wt_rdy = b.n("DVE")
                OUTQ = QS if which == "q" else KS
                cosT, sinT = (COSQ, SINQ) if which == "q" else (COSK, SINK)
                bank_dve = [0, 0]
                bank_act = [0, 0]
                st_bank_act = [0, 0]
                st_bank_pe = [0, 0]
                rh_free_pe = [0, 0]
                qf_free_dve = [0, 0]
                rhb_free = [0, 0]
                xs_free = [0, 0]
                tab_free = [0, 0]
                qb_st = [0, 0]
                for ti in range(NTT):
                    s = ti % 2
                    b.dma(xs[s][:],
                          XB.rearrange("(c p) t -> p c t", p=128)[:, :, TT * ti:TT * (ti + 1)],
                          "LX0", waits=[("PE", xs_free[s])])
                    ldx = b.n("LX0")
                    if L == 0:
                        b.dma(cos_sb[s][:], cosT[:, TT * ti:TT * (ti + 1)], "LX1",
                              waits=[("DVE", tab_free[s])])
                        b.dma(sin_sb[s][:], sinT[:, TT * ti:TT * (ti + 1)], "LX2",
                              waits=[("DVE", tab_free[s])])
                    ldc, lds = b.n("LX1"), b.n("LX2")
                    for ec in range(NDC):
                        k_ = ec % 2
                        pairs = [(wt_sb[:, kc, 128 * ec:128 * (ec + 1)],
                                  xs[s][:, kc, :]) for kc in range(NDC)]
                        mm = kaccum(mm_ps[k_][:], pairs,
                                    [("DVE", wt_rdy), ("LX0", ldx),
                                     ("DVE", bank_dve[k_]), ("ACT", bank_act[k_])])
                        if A1_SIMPLE_EPI:
                            a3 = b.op("vector", mk_copy(qb[k_][:], mm_ps[k_][:]),
                                      waits=[("PE", mm), ("ST0", qb_st[k_])],
                                      inc=("DVE", 1))
                            bank_dve[k_], bank_act[k_] = a3, 0
                        elif L == 0:
                            ev0 = b.op("scalar", mk_act(qf[k_][:], mm_ps[k_][:],
                                                        AF.Copy),
                                       waits=[("PE", mm),
                                              ("PE", rh_free_pe[k_]),
                                              ("DVE", qf_free_dve[k_])],
                                       inc=("ACT", 1))
                            rh = b.op("tensor", mk_mm(rh_ps[k_][:], perm_sb[:],
                                                      qf[k_][:], True, True),
                                      waits=[("ACT", ev0),
                                             ("DVE", rhb_free[k_]),
                                             ("LW", 16)],
                                      inc=("PE", 1))
                            a1 = b.op("vector", mk_tt(ALU.mult)(
                                ep1[k_][:], qf[k_][:], cos_sb[s][:]),
                                waits=[("ACT", ev0), ("LX1", ldc), ("LX2", lds),
                                       ("ST0", qb_st[k_])], inc=("DVE", 1))
                            a2 = b.op("vector", mk_tt(ALU.mult)(
                                ep2[k_][:], rh_ps[k_][:], sin_sb[s][:]),
                                waits=[("PE", rh)], inc=("DVE", 1))
                            a3 = b.op("vector", mk_tt(ALU.add)(
                                qb[k_][:], ep1[k_][:], ep2[k_][:]),
                                waits=[("DVE", a2)], inc=("DVE", 1))
                            rh_free_pe[k_] = rh
                            qf_free_dve[k_] = a1
                            rhb_free[k_] = a2
                            bank_dve[k_], bank_act[k_] = 0, ev0
                        else:
                            c0 = b.op("scalar", mk_act(ep1[k_][:], mm_ps[k_][:],
                                                       AF.Square),
                                      waits=[("PE", mm), ("ST0", qb_st[k_]),
                                             ("PE", st_bank_pe[k_])],
                                      inc=("ACT", 1))
                            pm = b.op("tensor", mk_mm(st_ps[k_][:], ones_col[:],
                                                      ep1[k_][:], True, True),
                                      waits=[("ACT", c0),
                                             ("ACT", st_bank_act[k_])],
                                      inc=("PE", 1))
                            c1 = b.op("scalar", mk_act(row1[k_][:], st_ps[k_][:],
                                                       AF.Sqrt, scale=1.0 / HD,
                                                       bias=eps_col[0:1, :]),
                                      waits=[("PE", pm)], inc=("ACT", 1))
                            st_bank_act[k_] = c1
                            d1 = b.op("vector",
                                      (lambda k2: lambda e: e.reciprocal(
                                          row1[k2][:], row1[k2][:]))(k_),
                                      waits=[("ACT", c1)], inc=("DVE", 1))
                            grow = gqs_row if which == "q" else gk_row
                            pb2 = b.op("tensor", mk_mm(bc_ps[k_][:], grow[:],
                                                       row1[k_][:], True, True),
                                       waits=[("DVE", d1),
                                              ("DVE", bank_dve[k_])],
                                       inc=("PE", 1))
                            c2 = b.op("scalar", mk_act(bcs[k_][:], bc_ps[k_][:],
                                                       AF.Copy),
                                      waits=[("PE", pb2)], inc=("ACT", 1))
                            a3 = b.op("vector", mk_tt(ALU.mult)(
                                qb[k_][:], mm_ps[k_][:], bcs[k_][:]),
                                waits=[("ACT", c2)], inc=("DVE", 1))
                            st_bank_pe[k_] = pb2
                            bank_dve[k_], bank_act[k_] = a3, c0
                        sto = b.dma(OUTQ[128 * ec:128 * (ec + 1),
                                         TT * ti:TT * (ti + 1)],
                                    qb[k_][:], "ST0", waits=[("DVE", a3)])
                        qb_st[k_] = sto
                    xs_free[s] = b.n("PE")
                    tab_free[s] = b.n("DVE")

            def sweep_v(W):
                plan_weight_transpose(W, wt_sb, wstg_f, wstg_b, tp_ps, b.n("PE"))
                wt_rdy = b.n("DVE")
                bank_act = [0, 0]
                xs_free = [0, 0]
                vb_st = [0, 0]
                for ti in range(NTT):
                    s = ti % 2
                    b.dma(xs[s][:],
                          XB.rearrange("(c p) t -> p c t", p=128)[:, :, TT * ti:TT * (ti + 1)],
                          "LX0", waits=[("PE", xs_free[s])])
                    ldx = b.n("LX0")
                    for tc4 in range(4):
                        tc = 4 * ti + tc4
                        for eg in range(4):
                            k_ = (tc4 * 4 + eg) % 2
                            pairs = [(xs[s][:, kc, 128 * tc4:128 * (tc4 + 1)],
                                      wt_sb[:, kc, 512 * eg:512 * (eg + 1)])
                                     for kc in range(NDC)]
                            mm = kaccum(mm_ps[k_][:], pairs,
                                        [("DVE", wt_rdy), ("LX0", ldx),
                                         ("ACT", bank_act[k_])])
                            ca = b.op("scalar", mk_act(qb[k_][:], mm_ps[k_][:],
                                                       AF.Copy),
                                      waits=[("PE", mm), ("ST0", vb_st[k_])],
                                      inc=("ACT", 1))
                            bank_act[k_] = ca
                            sto = 0
                            for hh in range(4):
                                h = 4 * eg + hh
                                sto = b.dma(VS[h, :, tc, :],
                                            qb[k_][:, 128 * hh:128 * (hh + 1)],
                                            "ST0", waits=[("ACT", ca)])
                            vb_st[k_] = sto
                    xs_free[s] = b.n("PE")

            if "t" in A1_PARTS:
                plan_weight_transpose(Wq, wt_sb, wstg_f, wstg_b, tp_ps, b.n("PE"))
            if "q" in A1_PARTS:
                sweep_qk(Wq, "q")
            if "k" in A1_PARTS:
                sweep_qk(Wk, "k")
            if "v" in A1_PARTS:
                sweep_v(Wv)
            b.fence()
            b.flush()

    def plan_a2(L):
        with ExitStack() as st_:
            E = st_.enter_context
            qh = [E(sbuf(f"a2_q{i}", [128, T], BF16)) for i in range(2)]
            kh = [E(sbuf(f"a2_k{i}", [128, T], BF16)) for i in range(2)]
            vh = [E(sbuf(f"a2_v{i}", [128, T // 128, HD], BF16)) for i in range(2)]
            s4 = E(sbuf("a2_s4", [128, 4, T], F32))
            p_sb = E(sbuf("a2_p", [128, T], BF16))
            pn_sb = [E(sbuf(f"a2_pn{i}", [128, T], BF16)) for i in range(2)]
            pt_sb = E(sbuf("a2_pt", [128, T // 128, TT], BF16))
            ob_sb = [E(sbuf(f"a2_ob{i}", [128, TT], BF16)) for i in range(2)]
            nm = [E(sbuf(f"a2_nm{i}", [128, 1], F32)) for i in range(2)]
            lsum = [E(sbuf(f"a2_l{i}", [128, 1], F32)) for i in range(2)]
            rl = [E(sbuf(f"a2_rl{i}", [128, 1], F32)) for i in range(2)]
            a2m = E(sbuf("a2_m", [128, 4, TT], F32))
            sc_ps = [E(psum(f"a2_sc{i}", [128, TT], F32)) for i in range(2)]
            pt_ps = [E(psum(f"a2_ptp{i}", [128, TT], BF16)) for i in range(2)]
            o_ps = [E(psum(f"a2_o{i}", [128, TT], F32)) for i in range(2)]

            b.op("gpsimd", lambda e: e.memset(a2m[:], 0.0), inc=("GP", 1))
            for d_i in range(4):
                b.op("gpsimd", (lambda di: (lambda e: e.affine_select(
                    out=a2m[:, di, :], in_=a2m[:, di, :], compare_op=ALU.is_ge,
                    fill=NEG, base=128 * di, pattern=[[-1, TT]],
                    channel_multiplier=1)))(d_i), inc=("GP", 1))
            mask_rdy = b.n("GP")
            head_free = [0, 0]
            o_free = [0, 0]
            ob_st = [0, 0]
            sc_free = [0, 0]
            ptb_free = [0, 0]
            tp_done = [0, 0]
            exp_done = [0, 0]
            for h in range(H):
                s = h % 2
                b.dma(qh[s][:], QS[128 * h:128 * (h + 1), :], "LX0",
                      waits=[("PE", head_free[s])])
                ldq = b.n("LX0")
                b.dma(kh[s][:], KS[128 * h:128 * (h + 1), :], "LX1",
                      waits=[("PE", head_free[s])])
                ldk = b.n("LX1")
                b.dma(vh[s][:], VS[h], "LX2", waits=[("PE", head_free[s])])
                ldv = b.n("LX2")
                def emit_scores(g, j):
                    ke = 512 * (g + 1)
                    ev = 0
                    for kt in range(g + 1):
                        k_ = kt % 2
                        mm = b.op("tensor", mk_mm(
                            sc_ps[k_][:],
                            qh[s][:, 512 * g + 128 * j:512 * g + 128 * (j + 1)],
                            kh[s][:, 512 * kt:512 * (kt + 1)], True, True),
                            waits=[("LX0", ldq), ("LX1", ldk),
                                   ("ACT", sc_free[k_])],
                            inc=("PE", 1))
                        ev = b.op("scalar", mk_act(
                            s4[:, j, 512 * kt:512 * (kt + 1)], sc_ps[k_][:],
                            AF.Copy), waits=[("PE", mm)], inc=("ACT", 1))
                        sc_free[k_] = ev
                    return ev

                def emit_softmax(g, j, ev):
                    ke = 512 * (g + 1)
                    jj = j % 2
                    mk_ = b.op("vector", mk_tt(ALU.add)(
                        s4[:, j, 512 * g:512 * (g + 1)],
                        s4[:, j, 512 * g:512 * (g + 1)], a2m[:, j, :]),
                        waits=[("ACT", ev), ("GP", mask_rdy)], inc=("DVE", 1))
                    mx = b.op("vector", (lambda kke, j2, jv: lambda e: e.reduce_max(
                        nm[j2][:], s4[:, jv, 0:kke], axis=AX.X, negate=True))(ke, jj, j),
                        waits=[("ACT", ev), ("ACT", exp_done[jj])],
                        fwaits=[("DVE", mk_)], inc=("DVE", 1))
                    ex = b.op("scalar", (lambda kke, j2, jv: lambda e: e.activation(
                        out=p_sb[:, 0:kke], in_=s4[:, jv, 0:kke], func=AF.Exp,
                        bias=nm[j2][:], accum_out=lsum[j2][:]))(ke, jj, j),
                        waits=[("DVE", mx)], inc=("ACT", 1))
                    exp_done[jj] = ex
                    rc = b.op("vector", (lambda j2: lambda e: e.reciprocal(
                        rl[j2][:], lsum[j2][:]))(jj),
                        waits=[("ACT", ex)], inc=("DVE", 1))
                    pn = b.op("vector", (lambda kke, j2: lambda e: e.tensor_scalar(
                        out=pn_sb[j2][:, 0:kke], in0=p_sb[:, 0:kke],
                        scalar1=rl[j2][:], scalar2=None, op0=ALU.mult))(ke, jj),
                        waits=[("PE", tp_done[jj])], fwaits=[("DVE", rc)],
                        inc=("DVE", 1))
                    for kb in range(g + 1):
                        t_ = kb % 2
                        tpl = None
                        for q4 in range(4):
                            kc = 4 * kb + q4
                            tpl = b.op("tensor", mk_tp(
                                pt_ps[t_][:, 128 * q4:128 * (q4 + 1)],
                                pn_sb[jj][:, 128 * kc:128 * (kc + 1)]),
                                waits=([("DVE", pn), ("ACT", ptb_free[t_])]
                                       if q4 == 0 else ()),
                                inc=("PE", 1) if q4 == 3 else None)
                        pv = b.op("scalar", mk_act(
                            pt_sb[:, 4 * kb:4 * kb + 4, 128 * j:128 * (j + 1)],
                            pt_ps[t_][:].rearrange("p (c x) -> p c x", x=128),
                            AF.Copy), waits=[("PE", tpl)], inc=("ACT", 1))
                        ptb_free[t_] = pv
                    tp_done[jj] = b.n("PE")

                def emit_av(g):
                    ke = 512 * (g + 1)
                    nkc = ke // 128
                    pt_rdy = b.n("ACT")
                    og = g % 2
                    pairs = [(vh[s][:, kc, :], pt_sb[:, kc, :]) for kc in range(nkc)]
                    mm = kaccum(o_ps[og][:], pairs,
                                [("ACT", pt_rdy), ("LX2", ldv), ("ACT", o_free[og])])
                    oc = b.op("scalar", mk_act(ob_sb[og][:], o_ps[og][:], AF.Copy),
                              waits=[("PE", mm), ("ST0", ob_st[og])], inc=("ACT", 1))
                    o_free[og] = oc
                    sto = b.dma(OS[128 * h:128 * (h + 1), 512 * g:512 * (g + 1)],
                                ob_sb[og][:], "ST0", waits=[("ACT", oc)])
                    ob_st[og] = sto

                pend = None
                for g in range(NQG):
                    for j in range(4):
                        ev = emit_scores(g, j)
                        if pend is not None:
                            emit_softmax(*pend)
                            if pend[1] == 3:
                                emit_av(pend[0])
                        pend = (g, j, ev)
                emit_softmax(*pend)
                emit_av(NQG - 1)
                head_free[s] = b.n("PE")
            b.fence()
            b.flush()

    def plan_a3(L):
        Wo = W0["O"] if L == 0 else W1["O"]
        resid = XIN if L == 0 else XF
        with ExitStack() as st_:
            E = st_.enter_context
            wt_sb = E(sbuf("a3_wt", [128, NDC, D], BF16))
            wstg_f = [E(sbuf(f"a3_wf{i}", [128, D], F32)) for i in range(2)]
            wstg_b = [E(sbuf(f"a3_wb{i}", [128, D], BF16)) for i in range(2)]
            os_t = [E(sbuf(f"a3_os{i}", [128, NDC, TT], BF16)) for i in range(2)]
            xr = [E(sbuf(f"a3_xr{i}", [128, TT], F32)) for i in range(2)]
            z_sb = E(sbuf("a3_z", [128, NDC, TT], F32))
            zsq = [E(sbuf(f"a3_zs{i}", [128, TT], F32)) for i in range(2)]
            rows = [E(sbuf(f"a3_row{i}", [1, TT], F32)) for i in range(4)]
            a_sb = E(sbuf("a3_a", [128, TT], F32))
            b_sb = E(sbuf("a3_b", [128, TT], F32))
            xn_sb = [E(sbuf(f"a3_xn{i}", [128, TT], F32)) for i in range(2)]
            xb_sb = [E(sbuf(f"a3_xb{i}", [128, TT], BF16)) for i in range(2)]
            tp_ps = [E(psum(f"a3_tp{i}", [128, 128], BF16)) for i in range(2)]
            y_ps = [E(psum(f"a3_y{i}", [128, TT], F32)) for i in range(2)]
            s1_ps = E(psum("a3_s1", [1, TT], F32))
            s2_ps = E(psum("a3_s2", [1, TT], F32))
            bc_ps = [E(psum(f"a3_bc{i}", [128, TT], F32)) for i in range(2)]

            plan_weight_transpose(Wo, wt_sb, wstg_f, wstg_b, tp_ps, b.n("PE"))
            wt_rdy = b.n("DVE")
            ln_state = {"stats_pe": 0, "stats_free": 0, "bc_free": 0,
                        "xn_st": [0, 0]}
            os_free = [0, 0]
            xr_free = [0, 0]
            bank_dve = [0, 0]
            zsq_free = [0, 0]
            for ti in range(NTT):
                s = ti % 2
                b.dma(os_t[s][:],
                      OS.rearrange("(c p) t -> p c t", p=128)[:, :, TT * ti:TT * (ti + 1)],
                      "LX0", waits=[("PE", os_free[s])])
                ldo = b.n("LX0")
                for ec in range(NDC):
                    k_ = ec % 2
                    b.dma(xr[k_][:],
                          resid[128 * ec:128 * (ec + 1), TT * ti:TT * (ti + 1)],
                          "LX1", waits=[("DVE", xr_free[k_])])
                    ldr = b.n("LX1")
                    pairs = [(wt_sb[:, kc, 128 * ec:128 * (ec + 1)],
                              os_t[s][:, kc, :]) for kc in range(NDC)]
                    mm = kaccum(y_ps[k_][:], pairs,
                                [("DVE", wt_rdy), ("LX0", ldo),
                                 ("DVE", bank_dve[k_])])
                    zz = b.op("vector", mk_tt(ALU.add)(
                        z_sb[:, ec, :], y_ps[k_][:], xr[k_][:]),
                        waits=[("PE", mm), ("LX1", ldr)], inc=("DVE", 1))
                    bank_dve[k_] = zz
                    xr_free[k_] = zz
                    sq = b.op("scalar", mk_act(zsq[k_][:], z_sb[:, ec, :],
                                               AF.Square),
                              waits=[("DVE", zz), ("PE", zsq_free[k_])],
                              inc=("ACT", 1))
                    b.op("tensor", mk_mm(s1_ps[:], ones_col[:], z_sb[:, ec, :],
                                         ec == 0, ec == NDC - 1),
                         waits=[("DVE", zz), ("ACT", ln_state["stats_free"])],
                         inc=("PE", 1))
                    p2 = b.op("tensor", mk_mm(s2_ps[:], ones_col[:], zsq[k_][:],
                                              ec == 0, ec == NDC - 1),
                              waits=[("ACT", sq),
                                     ("ACT", ln_state["stats_free"])],
                              inc=("PE", 1))
                    zsq_free[k_] = p2
                    ln_state["stats_pe"] = p2
                plan_ln_apply(ti, z_sb, s1_ps, s2_ps, bc_ps, rows, a_sb, b_sb,
                              XF, XB, xn_sb, xb_sb, ln_state)
                os_free[s] = b.n("PE")
            b.fence()
            b.flush()

    def plan_f1():
        jobs = [(IDXG, WGT, NFC, D, NDC), (IDXU, WUT, NFC, D, NDC),
                (IDXD, WDT, NDC, DFF, NFC)]
        with ExitStack() as st_:
            E = st_.enter_context
            ix_i = [E(sbuf(f"f1_ix{i}", [128, DFF], I32)) for i in range(2)]
            ix_b = [E(sbuf(f"f1_ib{i}", [128, DFF], BF16)) for i in range(2)]
            acc = E(sbuf("f1_acc", [128, DFF], BF16))
            tmp = E(sbuf("f1_tmp", [128, DFF], BF16))
            acc2 = E(sbuf("f1_acc2", [128, DFF], BF16))
            tmp2 = E(sbuf("f1_tmp2", [128, DFF], BF16))
            stg = [E(sbuf(f"f1_st{i}", [128, NFC, 128], BF16)) for i in range(2)]
            tp_ps = [E(psum(f"f1_tp{i}", [128, TT], BF16)) for i in range(2)]
            ld_free = [0, 0]
            cast_free = [0, 0]
            stg_st = [0, 0]
            ptb_free = [0, 0]
            acc_rd = 0
            acc2_rd = [0]
            for IDX, WT_, nR, Wd, nCC in jobs:
                for rc in range(nR):
                    s = rc % 2
                    b.dma(ix_i[s][:, 0:Wd], IDX[128 * rc:128 * (rc + 1), :], "LW",
                          waits=[("ACT", ld_free[s])])
                    ca = b.op("scalar", mk_act(ix_b[s][:, 0:Wd], ix_i[s][:, 0:Wd],
                                               AF.Copy),
                              waits=[("LW", b.n("LW")), ("DVE", cast_free[s])],
                              inc=("ACT", 1))
                    ld_free[s] = ca
                    g0 = b.op("vector", (lambda ss, ww: lambda e: e.tensor_scalar(
                        out=acc[:, 0:ww], in0=ix_b[ss][:, 0:ww], scalar1=0.0,
                        scalar2=lut_bc[:, 0:1], op0=ALU.is_equal,
                        op1=ALU.mult))(s, Wd),
                        waits=[("ACT", ca), ("PE", acc_rd)], inc=("DVE", 1))
                    for jv in range(1, 16):
                        b.op("vector", (lambda ss, jj, ww: lambda e: e.tensor_scalar(
                            out=tmp[:, 0:ww], in0=ix_b[ss][:, 0:ww],
                            scalar1=float(jj), scalar2=lut_bc[:, jj:jj + 1],
                            op0=ALU.is_equal, op1=ALU.mult))(s, jv, Wd),
                            inc=("DVE", 1))
                        g0 = b.op("vector", (lambda ww: lambda e: e.tensor_add(
                            acc[:, 0:ww], acc[:, 0:ww], tmp[:, 0:ww]))(Wd),
                            inc=("DVE", 1))
                    cast_free[s] = g0
                    pv = 0
                    for kb in range((nCC + 3) // 4):
                        t_ = kb % 2
                        n4 = min(4, nCC - 4 * kb)
                        tpl = None
                        for q4 in range(n4):
                            cc = 4 * kb + q4
                            tpl = b.op("tensor", mk_tp(
                                tp_ps[t_][:, 128 * q4:128 * (q4 + 1)],
                                acc[:, 128 * cc:128 * (cc + 1)]),
                                waits=([("DVE", g0), ("ACT", ptb_free[t_])]
                                       if q4 == 0 else ()),
                                inc=("PE", 1) if q4 == n4 - 1 else None)
                        pv = b.op("scalar", mk_act(
                            stg[s][:, 4 * kb:4 * kb + n4, :],
                            tp_ps[t_][:, 0:128 * n4].rearrange(
                                "p (c x) -> p c x", x=128),
                            AF.Copy),
                            waits=[("PE", tpl), ("ST0", stg_st[s])],
                            inc=("ACT", 1))
                        ptb_free[t_] = pv
                    acc_rd = b.n("PE")
                    sto = b.dma(WT_[rc][:, 0:nCC, :], stg[s][:, 0:nCC, :], "ST0",
                                waits=[("ACT", pv)])
                    stg_st[s] = sto
            b.fence()
            b.flush()

    def plan_f0():
        jobs = [(W1G, WGT, NDC, DFF), (W1U, WUT, NDC, DFF), (W1D, WDT, NFC, D),
                (BAUGG, WGT, None, DFF), (BAUGU, WUT, None, DFF),
                (BAUGD, WDT, None, D)]
        with ExitStack() as st_:
            E = st_.enter_context
            cf = [E(sbuf(f"f0_f{i}", [128, DFF], F32)) for i in range(2)]
            cb = [E(sbuf(f"f0_b{i}", [128, DFF], BF16)) for i in range(2)]
            ld_free = [0, 0]
            cb_st = [0, 0]
            for W_, WT_, nKC, Wd in jobs:
                extra = nKC is None
                kcs = [WT_.shape[2] - 1] if extra else list(range(nKC))
                for kc in kcs:
                    s = kc % 2
                    src_ap = W_[:] if extra else W_[128 * kc:128 * (kc + 1), :]
                    b.dma(cf[s][:, 0:Wd], src_ap, "LW",
                          waits=[("ACT", ld_free[s])])
                    ca = b.op("scalar", mk_act(cb[s][:, 0:Wd], cf[s][:, 0:Wd],
                                               AF.Copy),
                              waits=[("LW", b.n("LW")), ("ST0", cb_st[s])],
                              inc=("ACT", 1))
                    ld_free[s] = ca
                    sto = b.dma(WT_[:, :, kc, :].rearrange("c p x -> p c x"),
                                cb[s][:, 0:Wd].rearrange("p (c x) -> p c x", x=128),
                                "ST0", waits=[("ACT", ca)])
                    cb_st[s] = sto
            b.fence()
            b.flush()

    def plan_f2(L):
        with ExitStack() as st_:
            E = st_.enter_context
            xs = [E(sbuf(f"f2_xs{i}", [128, NDC, TT], BF16)) for i in range(2)]
            wg_t = [E(sbuf(f"f2_wg{i}", [128, NDC + 1, 128], BF16)) for i in range(2)]
            wu_t = [E(sbuf(f"f2_wu{i}", [128, NDC + 1, 128], BF16)) for i in range(2)]
            wd_t = [E(sbuf(f"f2_wd{i}", [128, NFC + 1, 128], BF16)) for i in range(2)]
            h_sb = E(sbuf("f2_h", [128, NFC, TT], BF16))
            sg_sb = [E(sbuf(f"f2_sg{i}", [128, TT], F32)) for i in range(2)]
            xr = [E(sbuf(f"f2_xr{i}", [128, TT], F32)) for i in range(2)]
            z_sb = E(sbuf("f2_z", [128, NDC, TT], F32))
            zsq = sg_sb
            rows = [E(sbuf(f"f2_row{i}", [1, TT], F32)) for i in range(4)]
            a_sb = E(sbuf("f2_a", [128, TT], F32))
            b_sb = E(sbuf("f2_b", [128, TT], F32))
            xn_sb = [E(sbuf(f"f2_xn{i}", [128, TT], F32)) for i in range(2)]
            xb_sb = [E(sbuf(f"f2_xb{i}", [128, TT], BF16)) for i in range(2)]
            gate_ps = [E(psum(f"f2_g{i}", [128, TT], F32)) for i in range(2)]
            up_ps = [E(psum(f"f2_u{i}", [128, TT], F32)) for i in range(2)]
            s1_ps = E(psum("f2_s1", [1, TT], F32))
            s2_ps = E(psum("f2_s2", [1, TT], F32))
            bc_ps = [E(psum(f"f2_bc{i}", [128, TT], F32)) for i in range(2)]

            ln_state = {"stats_pe": 0, "stats_free": 0, "bc_free": 0,
                        "xn_st": [0, 0]}
            if L == 1:
                wr_sb = E(sbuf("f2_wr", [128, NDC, 8], BF16))
                wrf = E(sbuf("f2_wrf", [128, NDC, 8], F32))
                augG = E(sbuf("f2_agG", [128, NDC, 128], BF16))
                augU = E(sbuf("f2_agU", [128, NDC, 128], BF16))
                augD = E(sbuf("f2_agD", [128, NFC, 128], BF16))
                augf = z_sb[:].rearrange("p c t -> p (c t)")[:, 0:NFC * 128]\
                    .rearrange("p (c x) -> p c x", x=128)
                r8 = E(sbuf("f2_r8", [128, 8], F32))
                r8b = E(sbuf("f2_r8b", [128, 8], F32))
                r8c = E(sbuf("f2_r8c", [128, 8], F32))
                m1c = E(sbuf("f2_m1", [128, 1], F32))
                w_bf = E(sbuf("f2_wbf", [128, 8], BF16))
                wt_row = E(sbuf("f2_wtr", [8, TT], BF16))
                w128 = E(sbuf("f2_w128", [128, TT], F32))
                zg_bf = E(sbuf("f2_zg", [128, TT], BF16))
                zu_bf = E(sbuf("f2_zu", [128, TT], BF16))
                zd_bf = E(sbuf("f2_zd", [128, TT], BF16))

                prev = 0
                for dst, stgf, srcap in [
                        (wr_sb[:], wrf[:], WR.rearrange("(c p) x -> p c x", p=128)),
                        (augG[:], augf[:, 0:NDC, :], AUGG.rearrange("(c p) x -> p c x", p=128)),
                        (augU[:], augf[:, 0:NDC, :], AUGU.rearrange("(c p) x -> p c x", p=128)),
                        (augD[:], augf[:], AUGD.rearrange("(c p) x -> p c x", p=128))]:
                    b.dma(stgf, srcap, "LW1", waits=[("ACT", prev)])
                    prev = b.op("scalar", mk_act(dst, stgf, AF.Copy),
                                waits=[("LW1", b.n("LW1"))], inc=("ACT", 1))
                l1_ready = prev
            else:
                l1_ready = 0

            resid = XF
            out_f32 = XF if L == 0 else OUT
            out_bf16 = XB if L == 0 else None
            xs_free = [0, 0]
            wg_free = [0, 0]
            wu_free = [0, 0]
            wdt_free = [0, 0]
            h_free = 0
            bankg_dve = [0, 0]
            bankg_act = [0, 0]
            banku_dve = [0, 0]
            xr_free = [0, 0]
            zsq_free = [0, 0]
            rt_sg_free = 0
            rt_lg_free = 0
            rt_tp_free = 0
            bc0_free_act = 0
            bc1_free_dve = 0
            for ti in range(NTT):
                s = ti % 2
                b.dma(xs[s][:],
                      XB.rearrange("(c p) t -> p c t", p=128)[:, :, TT * ti:TT * (ti + 1)],
                      "LX0", waits=[("PE", xs_free[s])])
                ldx = b.n("LX0")
                if L == 1:
                    wt_ps = bc_ps[1][:].bitcast(BF16)[0:8, 0:128]
                    for c4 in range(4):
                        pairs = [(xs[s][:, kc, 128 * c4:128 * (c4 + 1)],
                                  wr_sb[:, kc, :]) for kc in range(NDC)]
                        mm = kaccum(bc_ps[0][:, 0:8], pairs,
                                    [("LX0", ldx), ("ACT", l1_ready),
                                     ("ACT", rt_lg_free),
                                     ("ACT", ln_state["bc_free"])])
                        sg = b.op("scalar", mk_act(r8[:], bc_ps[0][:, 0:8],
                                                   AF.Sigmoid),
                                  waits=[("PE", mm), ("DVE", rt_sg_free)],
                                  inc=("ACT", 1))
                        rt_lg_free = sg
                        v1 = b.op("vector", mk_tt(ALU.add)(r8b[:], r8[:], ebias_bc[:]),
                                  waits=[("ACT", sg)], inc=("DVE", 1))
                        v2 = b.op("vector", lambda e: e.reduce_max(m1c[:], r8b[:],
                                                                   axis=AX.X),
                                  fwaits=[("DVE", v1)], inc=("DVE", 1))
                        v3 = b.op("vector", lambda e: e.tensor_scalar(
                            out=r8c[:], in0=r8b[:], scalar1=m1c[:], scalar2=None,
                            op0=ALU.is_ge), fwaits=[("DVE", v2)], inc=("DVE", 1))
                        v4 = b.op("vector", lambda e: e.scalar_tensor_tensor(
                            out=r8b[:], in0=r8c[:], scalar=NEG, in1=r8b[:],
                            op0=ALU.mult, op1=ALU.add), fwaits=[("DVE", v3)],
                            inc=("DVE", 1))
                        v5 = b.op("vector", lambda e: e.reduce_max(m1c[:], r8b[:],
                                                                   axis=AX.X),
                                  fwaits=[("DVE", v4)], inc=("DVE", 1))
                        v6 = b.op("vector", lambda e: e.tensor_scalar(
                            out=r8b[:], in0=r8b[:], scalar1=m1c[:], scalar2=None,
                            op0=ALU.is_ge), fwaits=[("DVE", v5)], inc=("DVE", 1))
                        v7 = b.op("vector", mk_tt(ALU.add)(r8c[:], r8c[:], r8b[:]),
                                  fwaits=[("DVE", v6)], inc=("DVE", 1))
                        v8 = b.op("vector", mk_tt(ALU.mult)(r8c[:], r8c[:], r8[:]),
                                  fwaits=[("DVE", v7)], inc=("DVE", 1))
                        rt_sg_free = v8
                        v9 = b.op("vector", lambda e: e.reduce_sum(m1c[:], r8c[:],
                                                                   axis=AX.X),
                                  fwaits=[("DVE", v8)], inc=("DVE", 1))
                        vA = b.op("vector", lambda e: e.tensor_scalar(
                            out=m1c[:], in0=m1c[:], scalar1=1e-8, scalar2=None,
                            op0=ALU.add), fwaits=[("DVE", v9)], inc=("DVE", 1))
                        vB = b.op("vector", lambda e: e.reciprocal(m1c[:], m1c[:]),
                                  fwaits=[("DVE", vA)], inc=("DVE", 1))
                        vC = b.op("vector", lambda e: e.tensor_scalar(
                            out=w_bf[:], in0=r8c[:], scalar1=m1c[:], scalar2=None,
                            op0=ALU.mult), fwaits=[("DVE", vB)], inc=("DVE", 1))
                        tp = b.op("tensor", mk_tp(wt_ps, w_bf[:]),
                                  waits=[("DVE", vC), ("ACT", rt_tp_free),
                                         ("DVE", bc1_free_dve),
                                         ("ACT", ln_state["bc_free"])],
                                  inc=("PE", 1))
                        pv = b.op("scalar", (lambda cc, src: lambda e: e.activation(
                            out=wt_row[:, 128 * cc:128 * (cc + 1)], in_=src,
                            func=AF.Copy))(c4, wt_ps),
                            waits=[("PE", tp)], inc=("ACT", 1))
                        rt_tp_free = pv
                    mmw = b.op("tensor", mk_mm(bc_ps[0][:], m2_sb[:], wt_row[:],
                                               True, True),
                               waits=[("ACT", b.n("ACT"))], inc=("PE", 1))
                    ew = b.op("scalar", mk_act(w128[:], bc_ps[0][:], AF.Copy),
                              waits=[("PE", mmw)], inc=("ACT", 1))
                    bc0_free_act = ew
                    ms = b.op("vector", lambda e: e.memset(w128[0:8, :], 0.125),
                              waits=[("ACT", ew)], inc=("DVE", 1))
                    zc_prev = 0
                    for aug_sb, z_out in ((augG, zg_bf), (augU, zu_bf)):
                        pairs = [(aug_sb[:, kc, :], xs[s][:, kc, :])
                                 for kc in range(NDC)]
                        mm = kaccum(bc_ps[1][:], pairs,
                                    [("ACT", rt_tp_free), ("DVE", bc1_free_dve)])
                        zz = b.op("vector", mk_tt(ALU.mult)(
                            sg_sb[0][:], bc_ps[1][:], w128[:]),
                            waits=[("PE", mm), ("DVE", ms), ("ACT", zc_prev)],
                            inc=("DVE", 1))
                        bc1_free_dve = zz
                        zc_prev = b.op("scalar", mk_act(z_out[:], sg_sb[0][:],
                                                        AF.Copy),
                                       waits=[("DVE", zz)], inc=("ACT", 1))
                    zready = zc_prev
                else:
                    zready = 0
                for dc in range(NFC):
                    k_ = dc % 2
                    b.dma(wg_t[k_][:], WGT[dc], "LW1", waits=[("PE", wg_free[k_])])
                    ldg = b.n("LW1")
                    b.dma(wu_t[k_][:], WUT[dc], "LW2", waits=[("PE", wu_free[k_])])
                    ldu = b.n("LW2")
                    gpairs = [(wg_t[k_][:, kc, :], xs[s][:, kc, :])
                              for kc in range(NDC)]
                    upairs = [(wu_t[k_][:, kc, :], xs[s][:, kc, :])
                              for kc in range(NDC)]
                    if L == 1:
                        gpairs.append((wg_t[k_][:, NDC, :], zg_bf[:]))
                        upairs.append((wu_t[k_][:, NDC, :], zu_bf[:]))
                    mmg = kaccum(gate_ps[k_][:], gpairs,
                                 [("LW1", ldg), ("LX0", ldx),
                                  ("DVE", bankg_dve[k_]), ("ACT", bankg_act[k_]),
                                  ("ACT", zready)])
                    mmu = kaccum(up_ps[k_][:], upairs,
                                 [("LW2", ldu), ("DVE", banku_dve[k_])])
                    sl = b.op("scalar", mk_act(sg_sb[k_][:], gate_ps[k_][:],
                                               AF.Silu),
                              waits=[("PE", mmg)], inc=("ACT", 1))
                    hh = b.op("vector", (lambda dd, k2: lambda e: e.tensor_tensor(
                        out=h_sb[:, dd, :], in0=sg_sb[k2][:], in1=up_ps[k2][:],
                        op=ALU.mult))(dc, k_),
                        waits=[("ACT", sl), ("PE", mmu), ("PE", h_free)],
                        inc=("DVE", 1))
                    wg_free[k_] = mmg
                    wu_free[k_] = mmu
                    bankg_dve[k_] = hh
                    bankg_act[k_] = sl
                    banku_dve[k_] = hh
                h_done = b.n("DVE")
                if "DH" in dbg and ti == 0:
                    b.dma(DH[:], h_sb[:], "ST0", waits=[("DVE", h_done)])
                if L == 1:
                    pairs = [(augD[:, kc, :], h_sb[:, kc, :]) for kc in range(NFC)]
                    mm = kaccum(bc_ps[1][:], pairs,
                                [("DVE", h_done), ("DVE", bc1_free_dve)])
                    zz = b.op("vector", mk_tt(ALU.mult)(
                        sg_sb[0][:], bc_ps[1][:], w128[:]),
                        waits=[("PE", mm), ("DVE", bankg_dve[0])], inc=("DVE", 1))
                    bc1_free_dve = zz
                    zdready = b.op("scalar", mk_act(zd_bf[:], sg_sb[0][:], AF.Copy),
                                   waits=[("DVE", zz)], inc=("ACT", 1))
                else:
                    zdready = 0
                for ec in range(NDC):
                    k_ = ec % 2
                    b.dma(wd_t[k_][:], WDT[ec], "LW1", waits=[("PE", wdt_free[k_])])
                    ldd = b.n("LW1")
                    b.dma(xr[k_][:],
                          resid[128 * ec:128 * (ec + 1), TT * ti:TT * (ti + 1)],
                          "LX1", waits=[("DVE", xr_free[k_])])
                    ldr = b.n("LX1")
                    dpairs = [(wd_t[k_][:, kc, :], h_sb[:, kc, :])
                              for kc in range(NFC)]
                    if L == 1:
                        dpairs.append((wd_t[k_][:, NFC, :], zd_bf[:]))
                    mm = kaccum(gate_ps[k_][:], dpairs,
                                [("LW1", ldd), ("DVE", h_done),
                                 ("DVE", bankg_dve[k_]), ("ACT", bankg_act[k_]),
                                 ("ACT", zdready)])
                    zz = b.op("vector", mk_tt(ALU.add)(
                        z_sb[:, ec, :], gate_ps[k_][:], xr[k_][:]),
                        waits=[("PE", mm), ("LX1", ldr)], inc=("DVE", 1))
                    bankg_dve[k_] = zz
                    bankg_act[k_] = 0
                    xr_free[k_] = zz
                    sq = b.op("scalar", mk_act(zsq[k_][:], z_sb[:, ec, :],
                                               AF.Square),
                              waits=[("DVE", zz), ("PE", zsq_free[k_])],
                              inc=("ACT", 1))
                    b.op("tensor", mk_mm(s1_ps[:], ones_col[:], z_sb[:, ec, :],
                                         ec == 0, ec == NDC - 1),
                         waits=[("DVE", zz), ("ACT", ln_state["stats_free"])],
                         inc=("PE", 1))
                    p2 = b.op("tensor", mk_mm(s2_ps[:], ones_col[:], zsq[k_][:],
                                              ec == 0, ec == NDC - 1),
                              waits=[("ACT", sq),
                                     ("ACT", ln_state["stats_free"])],
                              inc=("PE", 1))
                    zsq_free[k_] = p2
                    ln_state["stats_pe"] = p2
                    wdt_free[k_] = b.n("PE")
                if "DZ" in dbg and ti == 0:
                    b.dma(DZ[:], z_sb[:], "ST0", waits=[("DVE", b.n("DVE"))])
                ln_state["bc_free"] = max(ln_state["bc_free"], bc0_free_act)
                plan_ln_apply(ti, z_sb, s1_ps, s2_ps, bc_ps, rows, a_sb, b_sb,
                              out_f32, out_bf16, xn_sb, xb_sb, ln_state)
                bc1_free_dve = max(bc1_free_dve, 0)
                h_free = b.n("PE")
                xs_free[s] = b.n("PE")
            b.fence()
            b.flush()

    for L in (0, 1):
        if want(f"A1.{L}"):
            plan_a1(L)
        if want(f"A2.{L}"):
            plan_a2(L)
        if want(f"A3.{L}"):
            plan_a3(L)
        if L == 0 and want("F1"):
            plan_f1()
        if L == 1 and want("F0"):
            plan_f0()
        if want(f"F2.{L}"):
            plan_f2(L)

    stws = [(s, b.n(s)) for s in STORE_SEMS]
    b.op("sync", lambda e: e.nop(), waits=stws)
    b.emit()
    b.stack.close()
    return nc


def _host_consts():
    inv = 1.0 / (10000.0 ** (np.arange(0, HD, 2, dtype=np.float64) / HD))
    t = np.arange(T, dtype=np.float64)
    fr = np.outer(t, inv)
    emb = np.concatenate([fr, fr], axis=1)
    cos = np.cos(emb).T.astype(np.float32)
    sin = np.sin(emb).T.astype(np.float32)
    m2 = np.zeros((8, 128), dtype=np.float64)
    for e in range(8):
        m2[e, 64 + 8 * e:64 + 8 * (e + 1)] = 0.125
    permt = np.zeros((128, 128), dtype=np.float32)
    for jj in range(64):
        permt[jj + 64, jj] = -1.0
        permt[jj, jj + 64] = 1.0
    return {
        "PERMT": permt,
        "COSQ": np.ascontiguousarray(cos * np.float32(SCALE)),
        "SINQ": np.ascontiguousarray(sin * np.float32(SCALE)),
        "COSK": np.ascontiguousarray(cos),
        "SINK": np.ascontiguousarray(sin),
        "M2C": m2.astype(np.float32),
    }


def _host_inputs(inputs, core):
    f = np.float32

    def npf(x):
        return np.ascontiguousarray(np.asarray(x, dtype=f))

    def npi(x):
        return np.ascontiguousarray(np.asarray(x, dtype=np.int32))

    zg = np.zeros((D, 56), f)
    zb = np.zeros((56, DFF), f)
    zbd = np.zeros((56, D), f)
    d = {
        "XIN": np.ascontiguousarray(np.asarray(inputs["L0_attn_x"][core], f).T),
        "W0Q": npf(inputs["L0_attn_Wq"]), "W0K": npf(inputs["L0_attn_Wk"]),
        "W0V": npf(inputs["L0_attn_Wv"]), "W0O": npf(inputs["L0_attn_Wo"]),
        "LUT": npf(inputs["L0_ffn_lut"]).reshape(1, 16),
        "IDXG": npi(inputs["L0_ffn_gate_idx"]),
        "IDXU": npi(inputs["L0_ffn_up_idx"]),
        "IDXD": npi(inputs["L0_ffn_down_idx"]),
        "W1Q": npf(inputs["L1_attn_Wq"]), "W1K": npf(inputs["L1_attn_Wk"]),
        "W1V": npf(inputs["L1_attn_Wv"]), "W1O": npf(inputs["L1_attn_Wo"]),
        "GQ": npf(inputs["L1_attn_gq"]).reshape(1, HD),
        "GK": npf(inputs["L1_attn_gk"]).reshape(1, HD),
        "WR": npf(inputs["L1_ffn_W_router"]),
        "W1G": npf(inputs["L1_ffn_W_gate"]), "W1U": npf(inputs["L1_ffn_W_up"]),
        "W1D": npf(inputs["L1_ffn_W_down"]),
        "AUGG": np.ascontiguousarray(np.concatenate(
            [npf(inputs["L1_ffn_sA_gate"]), zg,
             npf(inputs["L1_ffn_A_gate"]).transpose(1, 0, 2).reshape(D, 64)],
            axis=1)),
        "AUGU": np.ascontiguousarray(np.concatenate(
            [npf(inputs["L1_ffn_sA_up"]), zg,
             npf(inputs["L1_ffn_A_up"]).transpose(1, 0, 2).reshape(D, 64)],
            axis=1)),
        "AUGD": np.ascontiguousarray(np.concatenate(
            [npf(inputs["L1_ffn_sA_down"]), np.zeros((DFF, 56), f),
             npf(inputs["L1_ffn_A_down"]).transpose(1, 0, 2).reshape(DFF, 64)],
            axis=1)),
        "BAUGG": np.ascontiguousarray(np.concatenate(
            [npf(inputs["L1_ffn_sB_gate"]), zb,
             npf(inputs["L1_ffn_B_gate"]).reshape(64, DFF)], axis=0)),
        "BAUGU": np.ascontiguousarray(np.concatenate(
            [npf(inputs["L1_ffn_sB_up"]), zb,
             npf(inputs["L1_ffn_B_up"]).reshape(64, DFF)], axis=0)),
        "BAUGD": np.ascontiguousarray(np.concatenate(
            [npf(inputs["L1_ffn_sB_down"]), zbd,
             npf(inputs["L1_ffn_B_down"]).reshape(64, D)], axis=0)),
        "EBIAS": npf(inputs["L1_ffn_expert_bias"]).reshape(1, 8),
    }
    return d


def kernel(**inputs):
    consts = _host_consts()
    nc = build_program()
    in_maps = []
    for c in range(8):
        m = _host_inputs(inputs, c)
        m.update(consts)
        in_maps.append(m)
    res = run_bass_kernel_spmd(nc, in_maps, core_ids=list(range(8)))
    out = np.stack([np.asarray(r["OUT"], dtype=np.float32).T
                    for r in res.results], axis=0)
    return np.ascontiguousarray(out)
